# version 12
# baseline (speedup 1.0000x reference)
import contextlib

COMPUTE = ("pe", "dve", "act", "pl")
SEQ_OF = {"pe": "tensor", "dve": "vector", "act": "scalar", "act_q": "scalar",
          "pl": "gpsimd", "pl_q": "gpsimd", "pl_cc": "gpsimd", "sp_q": "sync"}
DMA_POOL = 10


class Op:
    __slots__ = ("eng", "emit", "reads", "writes", "idx", "need_inc", "waits",
                 "sem", "inc_val", "tag", "seq")

    def __init__(self, eng, emit, reads, writes, tag):
        self.eng = eng
        self.emit = emit
        self.reads = reads
        self.writes = writes
        self.tag = tag
        self.need_inc = False
        self.waits = []


class Rec:
    def __init__(self):
        self.ops = []

    def add(self, eng, emit, reads=(), writes=(), tag=""):
        op = Op(eng, emit, list(reads), list(writes), tag)
        op.idx = len(self.ops)
        self.ops.append(op)
        return op

    def resolve(self):
        last_writer = {}
        readers = {}
        edges = {}
        for op in self.ops:
            deps = set()
            for r in op.reads:
                w = last_writer.get(r)
                if w is not None:
                    deps.add(w)
            for w_ in op.writes:
                w = last_writer.get(w_)
                if w is not None:
                    deps.add(w)
                for rd in readers.get(w_, ()):
                    deps.add(rd)
            deps.discard(op)
            edges[op.idx] = deps
            for r in op.reads:
                readers.setdefault(r, []).append(op)
            for w_ in op.writes:
                last_writer[w_] = op
                readers[w_] = []
        return edges

    def assign_and_emit(self, nc, block, st, verbose=False):
        edges = self.resolve()

        drain_before = set()
        for op in self.ops:
            for p in edges[op.idx]:
                same_stream = p.eng == op.eng
                if same_stream and op.eng in COMPUTE:
                    if op.eng != "pe":
                        drain_before.add((op.idx, op.eng))
                else:
                    p.need_inc = True

        sems = {}
        for e in COMPUTE:
            sems[e] = (st.enter_context(nc.semaphore(f"c_{e}")), )
        pool = {}
        for q in ("sp_q", "act_q", "pl_q"):
            pool[q] = [st.enter_context(nc.semaphore(f"{q}{i}"))
                       for i in range(DMA_POOL)]
        cc_sem = st.enter_context(nc.semaphore("cc_s"))

        cnt = {e: 0 for e in COMPUTE}
        pool_idx = {q: 0 for q in pool}
        pool_cnt = {}
        pool_last_op = {}
        cc_cnt = 0
        for op in self.ops:
            if op.eng in COMPUTE:
                if op.need_inc:
                    cnt[op.eng] += 1
                    op.sem = sems[op.eng][0]
                    op.inc_val = cnt[op.eng]
                else:
                    op.sem = None
            elif op.eng == "pl_cc":
                cc_cnt += 1
                op.sem = cc_sem
                op.inc_val = cc_cnt
                op.need_inc = True
            else:
                q = op.eng
                s = pool[q][pool_idx[q] % DMA_POOL]
                pool_idx[q] += 1
                prev = pool_last_op.get(id(s))
                if prev is not None:
                    op.waits.append((s, pool_cnt[id(s)]))
                pool_cnt[id(s)] = pool_cnt.get(id(s), 0) + 16
                pool_last_op[id(s)] = op
                op.sem = s
                op.inc_val = pool_cnt[id(s)]
                op.need_inc = True

        for op in self.ops:
            for p in sorted(edges[op.idx], key=lambda o: o.idx):
                same_stream = p.eng == op.eng
                if same_stream and op.eng in COMPUTE:
                    continue
                assert p.sem is not None, (p.tag, op.tag)
                op.waits.append((p.sem, p.inc_val))

        self.resolved_edges = edges
        by_seq = {}
        for op in self.ops:
            by_seq.setdefault(SEQ_OF[op.eng], []).append(op)

        stats = {"ops": len(self.ops), "incs": 0, "waits": 0, "drains": 0}
        for seq_name, ops in by_seq.items():
            eng = getattr(nc, {"tensor": "tensor", "vector": "vector",
                               "scalar": "scalar", "gpsimd": "gpsimd",
                               "sync": "sync"}[seq_name])
            deco = getattr(block, seq_name)

            def emit_all(e, ops=ops, seq_name=seq_name):
                waited = {}
                drained_at = -1
                prev_idx = -1
                for op in ops:
                    if (op.idx, op.eng) in drain_before:
                        prods = [p for p in self.resolved_edges[op.idx]
                                 if p.eng == op.eng]
                        if any(p.idx > drained_at for p in prods):
                            e.drain()
                            drained_at = prev_idx
                            stats["drains"] += 1
                    for s, v in op.waits:
                        if waited.get(id(s), -1) >= v:
                            continue
                        e.wait_ge(s, v)
                        waited[id(s)] = v
                        stats["waits"] += 1
                    instr = op.emit(e)
                    if op.need_inc:
                        k = 16 if op.eng in ("sp_q", "act_q", "pl_q") else 1
                        instr.then_inc(op.sem, k)
                        stats["incs"] += 1
                    prev_idx = op.idx

            deco(emit_all)
        return stats


"""Megakernel builder for mk_seq_postln_d1 (mamba2 mixer + postln + SwiGLU FFN).

Token-sharded: each core owns `t_own` tokens of one sequence (half-sequence in
the full config) plus a `halo`-token frame prefix (zeros at sequence start).
Sync edges are resolved by the recorder in rec.py.

Phase flow per core:
  setup -> P1 (in_proj + conv + dt) -> P1b (dt/A_cs prep) -> P2a (states)
  -> EX (AllGather + peer pick) -> P2b (Y + gating + RMS) -> P3 (out_proj+LN1)
  -> P4 (FFN + LN2) -> out

SBUF: two byte regions RA (136K) / RB (56K) aliased across phases + small
static tiles; dependency keys at 2KB granularity keep aliasing safe.
"""
import contextlib
import numpy as np
import concourse.bass as bass
import concourse.mybir as mybir

F32 = mybir.dt.float32
BF16 = mybir.dt.bfloat16
I32 = mybir.dt.int32
U8 = mybir.dt.uint8
AF = mybir.ActivationFunctionType
ALU = mybir.AluOpType

HID = 4096
H = 128
PDIM = 64
ND = 128
G = 8
KCONV = 4
CHK = 128
INTER = 8192
CONVD = 10240
PROJ = 18560
EPS = 1e-5
NFB = PROJ // 128
NKB = HID // 128
NEG = -1e30
BLK = 2048

RA_SZ = 139264
CAST_SLICES = (
    [("wout_bf", "wout_t", i * 8, (i + 1) * 8) for i in range(8)]
    + [("w12_bf", "w12_t", i * 4, (i + 1) * 4) for i in range(8)]
    + [("w3_bf", "w3_t", i * 8, (i + 1) * 8) for i in range(4)]
)
RB_SZ = 57344


class Cfg:
    def __init__(self, t_own=2048, halo=128, ncores=8, upto="p4", taps=(),
                 p2b_cut=99):
        self.p2b_cut = p2b_cut
        self.t_own = t_own
        self.halo = halo
        self.tf = t_own + halo
        self.nch = t_own // CHK
        self.ncores = ncores
        self.upto = upto
        self.taps = tuple(taps)
        self.groups = []
        off = 0
        while off < self.tf:
            g = min(512, self.tf - off)
            self.groups.append((off, g))
            off += g
        self.groups_own = []
        off = self.halo
        while off < self.tf:
            g = min(512, self.tf - off)
            self.groups_own.append((off, g))
            off += g
        self.groups_hbc = []
        off = self.halo - 3
        while off < self.tf:
            g = min(512, self.tf - off)
            self.groups_hbc.append((off, g))
            off += g

    def past(self, phase):
        order = ["p1", "p2a", "ex", "p2b", "p3", "p4"]
        return order.index(self.upto) >= order.index(phase)


def host_shared(inputs):
    sh = {}
    w = np.asarray(inputs["L0_attn_in_proj_w"])
    sh["win_t"] = np.ascontiguousarray(
        w.reshape(NFB, 128, NKB, 128).transpose(0, 3, 2, 1))
    sh["wout_t"] = np.ascontiguousarray(
        np.asarray(inputs["L0_attn_out_proj_w"]).T.reshape(64, 128, HID))
    w12 = np.asarray(inputs["L0_ffn_w12"])
    sh["w12_t"] = np.ascontiguousarray(
        w12.T.reshape(NKB, 128, 2, 32, 128).transpose(3, 0, 1, 2, 4)
        .reshape(32, NKB, 128, 256))
    sh["w3_t"] = np.ascontiguousarray(
        np.asarray(inputs["L0_ffn_w3"]).T.reshape(32, 128, HID))
    cw = np.asarray(inputs["L0_attn_conv_w"]).reshape(CONVD, KCONV)
    sh["conv_w2"] = np.ascontiguousarray(
        cw.reshape(80, 128, KCONV).transpose(1, 0, 2).reshape(128, 80 * KCONV))
    sh["conv_b2"] = np.ascontiguousarray(
        np.asarray(inputs["L0_attn_conv_b"]).reshape(80, 128).T)
    sh["dt_bias_c"] = np.ascontiguousarray(
        np.asarray(inputs["L0_attn_dt_bias"]).reshape(128, 1))
    sh["a_log_c"] = np.ascontiguousarray(
        np.asarray(inputs["L0_attn_A_log"]).reshape(128, 1))
    sh["d_row"] = np.ascontiguousarray(np.asarray(inputs["L0_attn_D"]).reshape(1, 128))
    sh["nw_fm"] = np.ascontiguousarray(
        np.asarray(inputs["L0_attn_norm_weight"]).reshape(64, 128).T)
    sh["b12_col"] = np.ascontiguousarray(
        np.asarray(inputs["L0_ffn_b12"]).reshape(64, 128).T)
    sh["b3_row"] = np.ascontiguousarray(np.asarray(inputs["L0_ffn_b3"]).reshape(1, HID))
    return sh


def host_core_map(cfg, inputs, core_id, sh):
    x = np.asarray(inputs["L0_attn_hidden_states"])
    B = x.shape[0]
    seq = core_id % B
    part = core_id // B
    t0 = part * cfg.t_own
    x_pad = np.zeros((cfg.tf, HID), np.float32)
    x_pad[cfg.halo:] = x[seq, t0:t0 + cfg.t_own]
    if t0 > 0:
        x_pad[:cfg.halo] = x[seq, t0 - cfg.halo:t0]
    peer = core_id - B if part > 0 else core_id
    m = dict(sh)
    m["x_pad"] = x_pad
    m["idx_gather"] = (peer * 128 + np.arange(128, dtype=np.int32)).reshape(128, 1)
    m["sel"] = np.full((128, 1), 1.0 if part > 0 else 0.0, np.float32)
    return m


class T:
    def __init__(self, ap, keys, base=0, region=None):
        self.ap = ap
        self.k = keys
        self.base = base
        self.region = region

    def __getitem__(self, sl):
        return self.ap[sl]

    def sub(self, lo, hi):
        if self.region is None:
            return self.k
        b0 = (self.base + lo) // BLK
        b1 = (self.base + hi - 1) // BLK + 1
        return [f"{self.region}:{b}" for b in range(b0, b1)]


class Region:
    def __init__(self, st, nc, name, nbytes):
        self.name = name
        self.nbytes = nbytes
        self.t = st.enter_context(nc.sbuf_tensor(name, [128, nbytes], U8))

    def carve(self, off, shape, dt):
        esz = {F32: 4, BF16: 2, I32: 4}[dt]
        assert len(shape) == 2
        nb = shape[1] * esz
        assert off + nb <= self.nbytes, (self.name, off, nb, self.nbytes)
        ap = self.t[:, off:off + nb].bitcast(dt)
        if shape[0] != 128:
            ap = ap[0:shape[0], :]
        keys = [f"{self.name}:{b}" for b in range(off // BLK, (off + nb - 1) // BLK + 1)]
        return T(ap, keys, base=off, region=self.name)


def sbt(st, nc, name, shape, dt):
    t = st.enter_context(nc.sbuf_tensor(name, list(shape), dt))
    return T(t[:], [name])


def bcast(ap, h, p):
    return ap.unsqueeze(2).broadcast_to((ap.shape[0], h, p))


def build(cfg):
    nc = bass.Bass("TRN2", target_bir_lowering=False, debug=False)
    io = {}
    tf, t_own, halo, nch = cfg.tf, cfg.t_own, cfg.halo, cfg.nch

    def din(name, shape, dt):
        io[name] = nc.dram_tensor(name, list(shape), dt, kind="ExternalInput").ap()

    din("x_pad", [tf, HID], F32)
    din("win_t", [NFB, 128, NKB, 128], F32)
    din("wout_t", [64, 128, HID], F32)
    din("w12_t", [32, NKB, 128, 256], F32)
    din("w3_t", [32, 128, HID], F32)
    din("conv_w2", [128, 80 * KCONV], F32)
    din("conv_b2", [128, 80], F32)
    din("dt_bias_c", [128, 1], F32)
    din("a_log_c", [128, 1], F32)
    din("d_row", [1, 128], F32)
    din("nw_fm", [128, 64], F32)
    din("b12_col", [128, 64], F32)
    din("b3_row", [1, HID], F32)
    din("idx_gather", [128, 1], I32)
    din("sel", [128, 1], F32)
    io["out"] = nc.dram_tensor("out", [t_own, HID], F32, kind="ExternalOutput").ap()

    def scr(name, shape, dt):
        io[name] = nc.dram_tensor(name, list(shape), dt, kind="Internal").ap()

    scr("x_bf", [tf, HID], BF16)
    scr("gate_scr", [INTER, t_own], BF16)
    scr("conv_scr", [CONVD, t_own], BF16)
    scr("hst_scr", [nch, 128, INTER], BF16)
    scr("nloc_scr", [nch, 128, INTER], BF16)
    scr("acs_scr", [nch, 128, 128], F32)
    scr("tcg_scr", [2 * nch, 128], F32)
    scr("s_in_scr", [128, INTER], BF16)
    scr("normed_scr", [INTER, t_own], BF16)
    scr("ln1tm_scr", [t_own, HID], F32)
    scr("ln1fm_scr", [HID, t_own], BF16)
    scr("wout_bf", [64, 128, HID], BF16)
    scr("w12_bf", [32, NKB, 128, 256], BF16)
    scr("w3_bf", [32, 128, HID], BF16)
    scr("snd", [128, INTER], BF16)
    scr("gth", [cfg.ncores * 128, INTER], BF16)

    taps = {}
    for name in cfg.taps:
        src = io[name]
        taps[name] = nc.dram_tensor("tap_" + name, list(src.shape), src.dtype,
                                    kind="ExternalOutput").ap()

    r = Rec()
    with contextlib.ExitStack() as st:
        RA = Region(st, nc, "RA", RA_SZ)
        RB = Region(st, nc, "RB", RB_SZ)

        dt_t = sbt(st, nc, "dt_t", [128, tf], F32)
        ident_bf = sbt(st, nc, "ident_bf", [128, 128], BF16)
        ident_f = sbt(st, nc, "ident_f", [128, 128], F32)
        ones_row = sbt(st, nc, "ones_row", [1, 128], F32)
        ones_col = sbt(st, nc, "ones_col", [128, 1], F32)
        dbc = sbt(st, nc, "dbc", [128, 128], F32)
        nw_sb = sbt(st, nc, "nw_sb", [128, 64], F32)
        conv_w_sb = sbt(st, nc, "conv_w_sb", [128, 80 * KCONV], F32)
        conv_b_sb = sbt(st, nc, "conv_b_sb", [128, 80], F32)
        dt_bias_sb = sbt(st, nc, "dt_bias_sb", [128, 1], F32)
        a_col = sbt(st, nc, "a_col", [128, 1], F32)
        b12_sb = sbt(st, nc, "b12_sb", [128, 64], F32)
        idx_sb = sbt(st, nc, "idx_sb", [128, 1], I32)
        sel_sb = sbt(st, nc, "sel_sb", [128, 1], F32)
        tc_cols = sbt(st, nc, "tc_cols", [128, max(nch, 2)], F32)
        gc_cols = sbt(st, nc, "gc_cols", [128, max(nch, 2)], F32)
        stats = sbt(st, nc, "stats", [128, 16], F32)

        pA = [T(st.enter_context(nc.psum_tensor(f"pA{i}", [128, 512], F32))[:],
                [f"pA{i}"]) for i in range(4)]
        pB = [T(st.enter_context(nc.psum_tensor(f"pB{i}", [128, 512], F32))[:],
                [f"pB{i}"]) for i in range(2)]
        pT = [T(st.enter_context(nc.psum_tensor(f"pT{i}", [128, 128], BF16))[:],
                [f"pT{i}"]) for i in range(2)]

        block = st.enter_context(nc.Block())

        d_row_sb = RB.carve(50176, [1, 128], F32)
        row_st = RB.carve(50688, [1, 128], F32)
        tct = RB.carve(51200, [max(nch, 2), 128], F32)

        o = nch * 512
        a_csT = RA.carve(0, [128, nch * 128], F32)
        dtT = RA.carve(o, [128, nch * 128], F32)
        exp_tc = RA.carve(2 * o, [128, nch * 128], F32)
        exp_gc = RA.carve(3 * o, [128, nch * 128], F32)
        gm_all = RA.carve(4 * o, [128, G * 128], BF16)
        pers_end = 4 * o + G * 512

        for nm, tile in [("conv_w2", conv_w_sb), ("conv_b2", conv_b_sb),
                         ("dt_bias_c", dt_bias_sb), ("nw_fm", nw_sb),
                         ("b12_col", b12_sb), ("idx_gather", idx_sb),
                         ("sel", sel_sb), ("d_row", d_row_sb),
                         ("a_log_c", a_col)]:
            r.add("sp_q", (lambda e, nm=nm, tile=tile:
                           e.dma_start(out=tile.ap, in_=io[nm][:])),
                  [], tile.k, f"ld_{nm}")

        for q in range(4):
            r.add("pl_q", (lambda e, q=q: e.dma_start(
                out=io["x_bf"][:, q * 1024:(q + 1) * 1024],
                in_=io["x_pad"][:, q * 1024:(q + 1) * 1024])),
                [], [f"x_bf{q}"], f"cast_x{q}")

        for tile, val in [(ident_bf, 1.0), (ident_f, 1.0), (ones_row, 1.0),
                          (ones_col, 1.0)]:
            r.add("pl", (lambda e, tile=tile, val=val: e.memset(tile.ap, val)),
                  [], tile.k, "memset")
        for tile in (ident_bf, ident_f):
            r.add("pl", (lambda e, tile=tile: e.affine_select(
                out=tile.ap, in_=tile.ap, compare_op=ALU.is_equal, fill=0.0,
                base=0, pattern=[[-1, 128]], channel_multiplier=1)),
                tile.k, tile.k, "ident")

        r.add("pe", lambda e: e.matmul(pB[0][:, 0:128], ones_row.ap, d_row_sb.ap,
                                       start=True, stop=True),
              ones_row.k + d_row_sb.k, pB[0].k, "dbc_mm")
        r.add("dve", lambda e: e.tensor_copy(out=dbc.ap, in_=pB[0][:, 0:128]),
              pB[0].k, dbc.k, "dbc_ev")

        x_fm = RA.carve(0, [128, NKB * tf], BF16)
        wsb = [RB.carve(0, [128, NKB * 128], BF16),
               RB.carve(8192, [128, NKB * 128], BF16)]
        gate_stg = [RB.carve(16384, [128, tf], BF16),
                    RB.carve(16384 + 2 * tf, [128, tf], BF16)]
        conv_in = RB.carve(16384 + 4 * tf, [128, tf], F32)
        conv_acc = RB.carve(16384 + 8 * tf, [128, t_own], F32)
        hbc_bf = [RB.carve(16384 + 8 * tf + 4 * t_own, [128, t_own], BF16),
                  RB.carve(16384 + 8 * tf + 6 * t_own, [128, t_own], BF16)]

        for kb in range(NKB):
            key = x_fm.k[(kb * tf * 2) // BLK: ((kb + 1) * tf * 2 - 1) // BLK + 1]
            r.add("sp_q", (lambda e, kb=kb: e.dma_start(
                out=x_fm[:, kb * tf:(kb + 1) * tf],
                in_=io["x_bf"][:, kb * 128:(kb + 1) * 128], transpose=True)),
                [f"x_bf{kb // 8}"], key, f"xtr{kb}")

        sp_tmp = RB.carve(53760, [128, 512], F32)
        adt_t = RB.carve(55808, [128, 128], F32)
        acs_c2 = RB.carve(56320, [128, 128], F32)

        def emit_p1b_early(env):
            bia = dt_bias_sb.ap[:, 0:1]
            for off, gsz in cfg.groups:
                r.add("act", (lambda e, off=off, gsz=gsz: e.activation(
                    out=pB[0].ap[:, 0:gsz], in_=dt_t[:, off:off + gsz],
                    func=AF.Relu, bias=bia, scale=1.0)),
                    dt_t.k + dt_bias_sb.k, pB[0].k, f"sp_relu{off}")
                r.add("act", (lambda e, off=off, gsz=gsz: e.activation(
                    out=sp_tmp.ap[:, 0:gsz], in_=dt_t[:, off:off + gsz],
                    func=AF.Abs, bias=bia, scale=1.0)),
                    dt_t.k + dt_bias_sb.k, sp_tmp.k, f"sp_abs{off}")
                r.add("act", (lambda e, gsz=gsz: e.activation(
                    out=sp_tmp.ap[:, 0:gsz], in_=sp_tmp.ap[:, 0:gsz],
                    func=AF.Exp, scale=-1.0)), sp_tmp.k, sp_tmp.k, f"sp_e{off}")
                r.add("act", (lambda e, gsz=gsz: e.activation(
                    out=sp_tmp.ap[:, 0:gsz], in_=sp_tmp.ap[:, 0:gsz],
                    func=AF.Ln, bias=1.0)), sp_tmp.k, sp_tmp.k, f"sp_l{off}")
                r.add("dve", (lambda e, off=off, gsz=gsz: e.tensor_tensor(
                    out=dt_t[:, off:off + gsz], in0=pB[0].ap[:, 0:gsz],
                    in1=sp_tmp.ap[:, 0:gsz], op=ALU.add)),
                    pB[0].k + sp_tmp.k, dt_t.k, f"sp_a{off}")
            r.add("act", lambda e: e.activation(out=a_col.ap, in_=a_col.ap,
                                                func=AF.Exp),
                  a_col.k, a_col.k, "a_exp")
            r.add("dve", lambda e: e.tensor_scalar(out=a_col.ap, in0=a_col.ap,
                                                   scalar1=-1.0, scalar2=None,
                                                   op0=ALU.mult),
                  a_col.k, a_col.k, "a_neg")
            for ci in range(nch):
                sl = slice(halo + ci * CHK, halo + (ci + 1) * CHK)
                r.add("dve", (lambda e, sl=sl: e.tensor_scalar(
                    out=adt_t.ap, in0=dt_t[:, sl], scalar1=a_col.ap,
                    scalar2=None, op0=ALU.mult)),
                    dt_t.k + a_col.k, adt_t.k, f"adt{ci}")
                r.add("dve", (lambda e: e.tensor_tensor_scan(
                    out=acs_c2.ap, data0=adt_t.ap, data1=adt_t.ap,
                    initial=0.0, op0=ALU.add, op1=ALU.bypass)),
                    adt_t.k, acs_c2.k, f"acs{ci}")
                r.add("sp_q", (lambda e, ci=ci: e.dma_start(
                    out=io["acs_scr"][ci], in_=acs_c2.ap)),
                    acs_c2.k, [f"acs_scr{ci}"], f"acsd{ci}")
                r.add("dve", (lambda e, ci=ci: e.tensor_copy(
                    out=tc_cols.ap[:, ci:ci + 1], in_=acs_c2.ap[:, 127:128])),
                    acs_c2.k, tc_cols.k, f"tc{ci}")
            r.add("dve", lambda e: e.memset(gc_cols.ap, 0.0), [], gc_cols.k, "gc0")
            if nch > 1:
                r.add("dve", lambda e: e.tensor_tensor_scan(
                    out=gc_cols.ap[:, 1:nch], data0=tc_cols.ap[:, 0:nch - 1],
                    data1=tc_cols.ap[:, 0:nch - 1], initial=0.0, op0=ALU.add,
                    op1=ALU.bypass), tc_cols.k, gc_cols.k, "gcscan")
            for j, srci in enumerate([tc_cols, gc_cols]):
                r.add("pe", (lambda e, srci=srci: e.transpose(
                    pB[1].ap[0:nch, 0:128], srci.ap[:, 0:nch], ident_f.ap)),
                    srci.k + ident_f.k, pB[1].k, f"tr_tcg{j}")
                r.add("dve", (lambda e: e.tensor_copy(out=tct.ap[0:nch, :],
                                                      in_=pB[1].ap[0:nch, 0:128])),
                      pB[1].k, tct.k, f"tcte{j}")
                r.add("sp_q", (lambda e, j=j: e.dma_start(
                    out=io["tcg_scr"][j * nch:(j + 1) * nch, :],
                    in_=tct.ap[0:nch, :])),
                    tct.k, [f"tcg{j}"], f"tcgd{j}")

        fb_order = [NFB - 1] + list(range(NFB - 1))
        for fbi, fb in enumerate(fb_order):
            slot = fbi % 2
            r.add("pl_q", (lambda e, fb=fb, slot=slot: e.dma_start(
                out=wsb[slot].ap,
                in_=io["win_t"][fb].rearrange("k kb m -> k (kb m)"))),
                [], wsb[slot].k, f"wld{fb}")
            kind = "gate" if fb < 64 else ("hbc" if fb < NFB - 1 else "dt")
            fb_groups = (cfg.groups_own if kind == "gate" else
                         (cfg.groups_hbc if kind == "hbc" else cfg.groups))
            for gi, (off, gsz) in enumerate(fb_groups):
                bank = pA[(fbi * 8 + gi) % 4]

                def mm_group(e, slot=slot, off=off, gsz=gsz, bank=bank):
                    for kb in range(NKB):
                        ins = e.matmul(bank.ap[:, 0:gsz],
                                       wsb[slot][:, kb * 128:(kb + 1) * 128],
                                       x_fm[:, kb * tf + off:kb * tf + off + gsz],
                                       start=(kb == 0), stop=(kb == NKB - 1))
                    return ins
                r.add("pe", mm_group, wsb[slot].k + x_fm.k, bank.k, f"p1mm{fb}.{gi}")
                if kind == "gate":
                    dtile, dk = gate_stg[slot], gate_stg[slot].k
                elif kind == "hbc":
                    dtile, dk = conv_in, conv_in.k
                else:
                    dtile, dk = dt_t, dt_t.k
                r.add("dve", (lambda e, dtile=dtile, off=off, gsz=gsz, bank=bank:
                              e.tensor_copy(out=dtile.ap[:, off:off + gsz],
                                            in_=bank.ap[:, 0:gsz])),
                      bank.k, dk, f"p1ev{fb}.{gi}")
            if kind == "gate":
                r.add("sp_q", (lambda e, fb=fb, slot=slot: e.dma_start(
                    out=io["gate_scr"].rearrange("(b p) t -> p b t", p=128)[:, fb, :],
                    in_=gate_stg[slot][:, halo:tf])),
                    gate_stg[slot].k, [f"gate_scr{fb // 8}"], f"gate_w{fb}")
            if fbi == 0:
                emit_p1b_early(locals())
            if fb >= 30 and fb % 5 == 0 and (fb - 30) // 5 < len(CAST_SLICES):
                nm_d, nm_s, i0_, i1_ = CAST_SLICES[(fb - 30) // 5]
                if (nm_d != "wout_bf" or cfg.past("p3")) and \
                   (nm_d == "wout_bf" or cfg.past("p4")):
                    r.add("pl_q", (lambda e, nm_d=nm_d, nm_s=nm_s, i0_=i0_,
                                   i1_=i1_: e.dma_start(
                        out=io[nm_d][i0_:i1_], in_=io[nm_s][i0_:i1_])),
                        [], [f"{nm_d}{i0_}"], f"cast_{nm_d}{i0_}")
            if kind == "hbc":
                cblk = fb - 64
                hslot = cblk % 2
                r.add("dve", (lambda e, cblk=cblk: e.tensor_scalar(
                    out=conv_acc.ap, in0=conv_in[:, halo - 3:halo - 3 + t_own],
                    scalar1=conv_w_sb.ap[:, cblk * KCONV:cblk * KCONV + 1],
                    scalar2=None, op0=ALU.mult)),
                    conv_in.k + conv_w_sb.k, conv_acc.k, f"cv0_{cblk}")
                for d in range(1, KCONV):
                    r.add("dve", (lambda e, cblk=cblk, d=d: e.scalar_tensor_tensor(
                        out=conv_acc.ap,
                        in0=conv_in[:, halo - 3 + d:halo - 3 + d + t_own],
                        scalar=conv_w_sb.ap[:, cblk * KCONV + d:cblk * KCONV + d + 1],
                        in1=conv_acc.ap, op0=ALU.mult, op1=ALU.add)),
                        conv_in.k + conv_acc.k + conv_w_sb.k, conv_acc.k,
                        f"cv{d}_{cblk}")
                r.add("act", (lambda e, cblk=cblk, hslot=hslot: e.activation(
                    out=hbc_bf[hslot].ap, in_=conv_acc.ap, func=AF.Silu,
                    bias=conv_b_sb.ap[:, cblk:cblk + 1], scale=1.0)),
                    conv_acc.k + conv_b_sb.k, hbc_bf[hslot].k, f"silu{cblk}")
                r.add("sp_q", (lambda e, cblk=cblk, hslot=hslot: e.dma_start(
                    out=io["conv_scr"].rearrange("(b p) t -> p b t", p=128)[:, cblk, :],
                    in_=hbc_bf[hslot].ap)),
                    hbc_bf[hslot].k, [f"conv_scr{cblk // 8}"], f"conv_w{cblk}")

        for ci in range(nch):
            r.add("sp_q", (lambda e, ci=ci: e.dma_start(
                out=acs_c2.ap, in_=io["acs_scr"][ci])),
                [f"acs_scr{ci}"], acs_c2.k, f"acsl{ci}")
            r.add("pe", (lambda e, ci=ci: e.transpose(
                pB[0][:, 0:128], dt_t[:, halo + ci * CHK:halo + (ci + 1) * CHK],
                ident_f.ap)), dt_t.k + ident_f.k, pB[0].k, f"dtT{ci}")
            r.add("dve", (lambda e, ci=ci: e.tensor_copy(
                out=dtT.ap[:, ci * 128:(ci + 1) * 128], in_=pB[0][:, 0:128])),
                pB[0].k, dtT.k, f"dtTe{ci}")
            r.add("pe", (lambda e, ci=ci: e.transpose(
                pB[1][:, 0:128], acs_c2.ap, ident_f.ap)),
                acs_c2.k + ident_f.k, pB[1].k, f"acsT{ci}")
            r.add("dve", (lambda e, ci=ci: e.tensor_copy(
                out=a_csT.ap[:, ci * 128:(ci + 1) * 128], in_=pB[1][:, 0:128])),
                pB[1].k, a_csT.k, f"acsTe{ci}")
        for j, dst in enumerate([exp_tc, exp_gc]):
            for ci in range(nch):
                r.add("sp_q", (lambda e, j=j, ci=ci: e.dma_start(
                    out=row_st.ap,
                    in_=io["tcg_scr"][j * nch + ci:j * nch + ci + 1, :])),
                    [f"tcg{j}"], row_st.k, f"rowld{j}_{ci}")
                r.add("pe", (lambda e: e.matmul(pB[1][:, 0:128], ones_row.ap,
                                                row_st.ap, start=True, stop=True)),
                      ones_row.k + row_st.k, pB[1].k, f"bctcg{j}_{ci}")
                r.add("act", (lambda e, dst=dst, ci=ci: e.activation(
                    out=dst.ap[:, ci * 128:(ci + 1) * 128], in_=pB[1][:, 0:128],
                    func=AF.Exp)), pB[1].k, dst.k, f"exptcg{j}_{ci}")
        r.add("dve", lambda e: e.tensor_scalar(out=exp_gc.ap, in0=exp_gc.ap,
                                               scalar1=sel_sb.ap, scalar2=None,
                                               op0=ALU.mult),
              exp_gc.k + sel_sb.k, exp_gc.k, "gcsel")
        if cfg.past("p2b"):
            mask01 = RA.carve(4 * (nch * 512) + G * 512 + 98816, [128, 128], F32)
            r.add("pl", lambda e: e.memset(mask01.ap, 1.0), [], mask01.k, "mk0")
            r.add("pl", (lambda e: e.affine_select(
                out=mask01.ap, in_=mask01.ap, compare_op=ALU.is_ge, fill=0.0,
                base=0, pattern=[[1, 128]], channel_multiplier=-1)),
                mask01.k, mask01.k, "mk1")

        if cfg.past("p2a"):
            hs_fm = RA.carve(pers_end, [128, INTER], BF16)
            hsT = RA.carve(pers_end + 16384, [128, INTER], BF16)
            xdec = RA.carve(pers_end + 32768, [128, INTER], BF16)
            s_stage = xdec
            s_master = RA.carve(pers_end + 49152, [128, INTER], F32)
            bc_fm = RA.carve(pers_end + 81920, [128, 2 * G * 128], BF16)
            b_T = RA.carve(pers_end + 86016, [128, G * 128], BF16)
            s_tmp = RA.carve(pers_end + 88064, [128, 512], F32)
            dec_t = RA.carve(pers_end + 90112, [128, 128], F32)
            dd_t = RA.carve(pers_end + 90624, [128, 128], F32)

            for hb in range(4):
                r.add("dve", (lambda e, hb=hb: e.memset(
                    s_master.ap[:, hb * 2048:(hb + 1) * 2048], 0.0)),
                    [], s_master.k, "s0")

            for ci in range(nch):
                for q in range(4):
                    r.add("sp_q", (lambda e, ci=ci, q=q: e.dma_start(
                        out=hs_fm[:, q * 2048:(q + 1) * 2048].rearrange(
                            "p (b t) -> p b t", t=CHK),
                        in_=io["conv_scr"].rearrange("(b p) t -> p b t", p=128)
                        [:, q * 16:(q + 1) * 16, ci * CHK:(ci + 1) * CHK])),
                        [f"conv_scr{q * 2}", f"conv_scr{q * 2 + 1}"],
                        hs_fm.sub(q * 4096, (q + 1) * 4096),
                        f"hsld{ci}.{q}")
                r.add("sp_q", (lambda e, ci=ci: e.dma_start(
                    out=bc_fm.ap.rearrange("p (b t) -> p b t", t=CHK),
                    in_=io["conv_scr"].rearrange("(b p) t -> p b t", p=128)
                    [:, 64:80, ci * CHK:(ci + 1) * CHK])),
                    ["conv_scr8", "conv_scr9"], bc_fm.k, f"bcld{ci}")
                r.add("dve", (lambda e: e.tensor_copy(out=s_stage.ap,
                                                      in_=s_master.ap)),
                      s_master.k, s_stage.k, f"sprev{ci}")
                r.add("sp_q", (lambda e, ci=ci: e.dma_start(
                    out=io["nloc_scr"][ci], in_=s_stage.ap)),
                    s_stage.k, [f"nloc_scr{ci}"], f"nlocw{ci}")
                for b2 in range(64):
                    bank = pT[b2 % 2]
                    r.add("pe", (lambda e, b2=b2, bank=bank: e.transpose(
                        bank.ap, hs_fm[:, b2 * 128:(b2 + 1) * 128], ident_bf.ap)),
                        hs_fm.sub(b2 * 256, (b2 + 1) * 256) + ident_bf.k,
                        bank.k, f"htr{ci}.{b2}")
                    r.add("dve" if b2 % 2 == 0 else "act",
                          (lambda e, b2=b2, bank=bank, d=(b2 % 2 == 0):
                           (e.tensor_copy(out=hsT[:, b2 * 128:(b2 + 1) * 128],
                                          in_=bank.ap) if d else
                            e.copy(out=hsT[:, b2 * 128:(b2 + 1) * 128],
                                   in_=bank.ap))),
                          bank.k, hsT.k, f"htre{ci}.{b2}")
                r.add("sp_q", (lambda e, ci=ci: e.dma_start(
                    out=io["hst_scr"][ci], in_=hsT.ap)),
                    hsT.k, [f"hst_scr{ci}"], f"hstw{ci}")
                r.add("sp_q", (lambda e, ci=ci: e.dma_start(
                    out=dec_t.ap, in_=io["acs_scr"][ci])),
                    [f"acs_scr{ci}"], dec_t.k, f"decl{ci}")
                r.add("dve", (lambda e, ci=ci: e.tensor_scalar(
                    out=dec_t.ap, in0=dec_t.ap,
                    scalar1=tc_cols.ap[:, ci:ci + 1], scalar2=None,
                    op0=ALU.subtract)),
                    dec_t.k + tc_cols.k, dec_t.k, f"dec{ci}")
                r.add("act", (lambda e: e.activation(out=dec_t.ap, in_=dec_t.ap,
                                                     func=AF.Exp, scale=-1.0)),
                      dec_t.k, dec_t.k, f"dece{ci}")
                r.add("pe", (lambda e: e.transpose(pB[0][:, 0:128], dec_t.ap,
                                                   ident_f.ap)),
                      dec_t.k + ident_f.k, pB[0].k, f"dtr{ci}")
                r.add("dve", (lambda e: e.tensor_copy(out=dec_t.ap,
                                                      in_=pB[0][:, 0:128])),
                      pB[0].k, dec_t.k, f"dtre{ci}")
                r.add("dve", (lambda e, ci=ci: e.tensor_tensor(
                    out=dd_t.ap, in0=dtT[:, ci * 128:(ci + 1) * 128],
                    in1=dec_t.ap, op=ALU.mult)),
                    dtT.k + dec_t.k, dd_t.k, f"dd{ci}")
                r.add("dve", (lambda e: e.tensor_tensor(
                    out=xdec.ap.rearrange("p (h q) -> p h q", q=PDIM),
                    in0=hsT.ap.rearrange("p (h q) -> p h q", q=PDIM),
                    in1=bcast(dd_t.ap, 128, PDIM), op=ALU.mult)),
                    hsT.k + dd_t.k, xdec.k, f"xdec{ci}")
                for g in range(G):
                    bank = pT[g % 2]
                    r.add("pe", (lambda e, g=g, bank=bank: e.transpose(
                        bank.ap, bc_fm[:, g * 128:(g + 1) * 128], ident_bf.ap)),
                        bc_fm.sub(g * 256, (g + 1) * 256) + ident_bf.k,
                        bank.k, f"btr{ci}.{g}")
                    r.add("dve", (lambda e, g=g, bank=bank: e.tensor_copy(
                        out=b_T[:, g * 128:(g + 1) * 128], in_=bank.ap)),
                        bank.k, b_T.k, f"btre{ci}.{g}")
                for hb in range(16):
                    bank = pA[hb % 2]

                    def smm(e, hb=hb, bank=bank):
                        for h2 in range(8):
                            h = hb * 8 + h2
                            ins = e.matmul(
                                bank.ap[:, h2 * 64:(h2 + 1) * 64],
                                b_T[:, (h % G) * 128:(h % G + 1) * 128],
                                xdec[:, h * 64:(h + 1) * 64],
                                start=(h2 == 0), stop=(h2 == 7))
                        return ins
                    r.add("pe", smm, b_T.k + xdec.k, bank.k, f"smm{ci}.{hb}")
                    r.add("dve", (lambda e, ci=ci, hb=hb: e.tensor_tensor(
                        out=s_tmp.ap.rearrange("p (h q) -> p h q", q=PDIM),
                        in0=s_master.ap[:, hb * 512:(hb + 1) * 512].rearrange(
                            "p (h q) -> p h q", q=PDIM),
                        in1=bcast(exp_tc.ap[:, ci * 128 + hb * 8:
                                            ci * 128 + (hb + 1) * 8], 8, PDIM),
                        op=ALU.mult)),
                        s_master.k + exp_tc.k, s_tmp.k, f"sdecay{ci}.{hb}")
                    r.add("dve", (lambda e, hb=hb, bank=bank: e.tensor_tensor(
                        out=s_master.ap[:, hb * 512:(hb + 1) * 512],
                        in0=s_tmp.ap, in1=bank.ap, op=ALU.add)),
                        s_tmp.k + bank.k, s_master.k, f"supd{ci}.{hb}")

        if cfg.past("ex"):
            r.add("dve", lambda e: e.tensor_copy(out=s_stage.ap, in_=s_master.ap),
                  s_master.k, s_stage.k, "sfin")
            r.add("sp_q", lambda e: e.dma_start(out=io["snd"][:], in_=s_stage.ap),
                  s_stage.k, ["snd"], "sndw")
            r.add("pl_cc", lambda e: e.collective_compute(
                "AllGather", ALU.bypass,
                replica_groups=[list(range(cfg.ncores))],
                ins=[io["snd"].opt()], outs=[io["gth"].opt()]),
                ["snd"], ["gth"], "allgather")
            r.add("pl_q", lambda e: e.indirect_dma_start(
                out=s_stage.ap, out_offset=None, in_=io["gth"][:],
                in_offset=bass.IndirectOffsetOnAxis(ap=idx_sb.ap, axis=0),
                bounds_check=cfg.ncores * 128),
                ["gth"] + idx_sb.k, s_stage.k, "gather")
            r.add("sp_q", lambda e: e.dma_start(out=io["s_in_scr"][:],
                                                in_=s_stage.ap),
                  s_stage.k, ["s_in_scr"], "s_in_w")

        if cfg.past("p2b"):
            r_c = RA.carve(pers_end, [128, INTER], BF16)
            hsT2 = RA.carve(pers_end + 16384, [128, INTER], BF16)
            x_ch = RA.carve(pers_end + 32768, [128, INTER], BF16)
            gate_ch = RA.carve(pers_end + 49152, [128, INTER], BF16)
            xg = RA.carve(pers_end + 65536, [128, INTER], BF16)
            nloc_t = RA.carve(pers_end + 81920, [128, INTER], BF16)
            mt_b = RA.carve(pers_end + 81920, [128, 2048], BF16)
            acs_t1_b = RA.carve(pers_end + 86016, [128, 2048], F32)
            expacsT = RA.carve(pers_end + 98304, [128, 128], F32)
            normed_st = RB.carve(0, [128, INTER], BF16)
            mt = RB.carve(16384, [128, 2048], BF16)
            et_d2 = [RB.carve(20480, [128, 2048], BF16),
                     RB.carve(24576, [128, 2048], BF16)]
            acs_t1 = RB.carve(28672, [128, 2048], F32)
            bc2 = RB.carve(36864, [128, 2 * G * 128], BF16)
            y_half_d = [RB.carve(40960, [128, 512], F32),
                        RB.carve(43008, [128, 512], F32)]
            sq_st = RB.carve(47104, [128, 1024], F32)
            rbc = RB.carve(47104, [128, 1024], F32)
            rstd = RB.carve(51200, [1, 1024], F32)

            mt_d = [mt, mt_b]
            acs_d = [acs_t1, acs_t1_b]

            def load_chunk(ci):
                for q in range(4):
                    r.add("sp_q", (lambda e, ci=ci, q=q: e.dma_start(
                        out=gate_ch[:, q * 2048:(q + 1) * 2048].rearrange(
                            "p (b t) -> p b t", t=CHK),
                        in_=io["gate_scr"].rearrange("(b p) t -> p b t", p=128)
                        [:, q * 16:(q + 1) * 16, ci * CHK:(ci + 1) * CHK])),
                        [f"gate_scr{q * 2}", f"gate_scr{q * 2 + 1}"],
                        gate_ch.sub(q * 4096, (q + 1) * 4096), f"gld{ci}.{q}")
                r.add("sp_q", (lambda e, ci=ci: e.dma_start(
                    out=hsT2.ap, in_=io["hst_scr"][ci])),
                    [f"hst_scr{ci}"], hsT2.k, f"hld{ci}")
                r.add("sp_q", (lambda e, ci=ci: e.dma_start(
                    out=bc2.ap.rearrange("p (b t) -> p b t", t=CHK),
                    in_=io["conv_scr"].rearrange("(b p) t -> p b t", p=128)
                    [:, 64:80, ci * CHK:(ci + 1) * CHK])),
                    ["conv_scr8", "conv_scr9"], bc2.k, f"bld{ci}")
                r.add("pl_q", (lambda e: e.dma_start(out=r_c.ap,
                                                     in_=io["s_in_scr"][:])),
                      ["s_in_scr"], r_c.k, f"rld{ci}")
                r.add("pl_q", (lambda e, ci=ci: e.dma_start(
                    out=nloc_t.ap, in_=io["nloc_scr"][ci])),
                    [f"nloc_scr{ci}"], nloc_t.k, f"nlld{ci}")

            def prep_chunk(ci):
                if cfg.p2b_cut >= 12:
                    r.add("pl", (lambda e, ci=ci: e.tensor_tensor(
                        out=r_c.ap.rearrange("p (h q) -> p h q", q=PDIM),
                        in0=r_c.ap.rearrange("p (h q) -> p h q", q=PDIM),
                        in1=bcast(exp_gc.ap[:, ci * 128:(ci + 1) * 128], 128,
                                  PDIM),
                        op=ALU.mult)),
                        r_c.k + exp_gc.k, r_c.k, f"rgam{ci}")
                if cfg.p2b_cut >= 14:
                    r.add("pl_q", (lambda e, ci=ci: e.dma_start(
                        out=nloc_t.ap, in_=io["nloc_scr"][ci])),
                        [f"nloc_scr{ci}"], nloc_t.k, f"nlld{ci}")
                    r.add("pl", (lambda e: e.tensor_tensor(
                        out=r_c.ap, in0=r_c.ap, in1=nloc_t.ap, op=ALU.add)),
                        r_c.k + nloc_t.k, r_c.k, f"radd{ci}")
                if cfg.p2b_cut >= 16:
                    r.add("dve", (lambda e, ci=ci: e.tensor_tensor(
                        out=x_ch.ap.rearrange("p (h q) -> p h q", q=PDIM),
                        in0=hsT2.ap.rearrange("p (h q) -> p h q", q=PDIM),
                        in1=bcast(dtT.ap[:, ci * 128:(ci + 1) * 128], 128, PDIM),
                        op=ALU.mult)),
                        hsT2.k + dtT.k, x_ch.k, f"xch{ci}")
                    r.add("dve", (lambda e: e.tensor_tensor(
                        out=hsT2.ap.rearrange("p (h q) -> p h q", q=PDIM),
                        in0=hsT2.ap.rearrange("p (h q) -> p h q", q=PDIM),
                        in1=bcast(dbc.ap[:, 0:128], 128, PDIM),
                        op=ALU.mult)),
                        hsT2.k + dbc.k, hsT2.k, f"hsd{ci}")
                if cfg.p2b_cut >= 18:
                    r.add("act", (lambda e, ci=ci: e.activation(
                        out=expacsT.ap, in_=a_csT[:, ci * 128:(ci + 1) * 128],
                        func=AF.Exp)), a_csT.k, expacsT.k, f"eat{ci}")
                    r.add("act", (lambda e: e.activation(out=gate_ch.ap,
                                                         in_=gate_ch.ap,
                                                         func=AF.Silu)),
                          gate_ch.k, gate_ch.k, f"sgate{ci}")
                if cfg.p2b_cut < 20:
                    return
                for g in range(G):
                    bank = pB[g % 2]
                    r.add("pe", (lambda e, g=g, bank=bank: e.matmul(
                        bank.ap[:, 0:128], bc2[:, g * 128:(g + 1) * 128],
                        bc2[:, (G + g) * 128:(G + g + 1) * 128],
                        start=True, stop=True)),
                        bc2.k, bank.k, f"gm{ci}.{g}")
                    r.add("act", (lambda e, g=g, bank=bank: e.copy(
                        out=gm_all.ap[:, g * 128:(g + 1) * 128],
                        in_=bank.ap[:, 0:128])),
                        bank.k, gm_all.k, f"gme{ci}.{g}")
                r.add("dve", (lambda e: e.tensor_tensor(
                    out=gm_all.ap.rearrange("p (g l) -> p g l", l=128),
                    in0=gm_all.ap.rearrange("p (g l) -> p g l", l=128),
                    in1=mask01.ap.unsqueeze(1).broadcast_to((128, G, 128)),
                    op=ALU.mult)),
                    gm_all.k + mask01.k, gm_all.k, f"gmmask{ci}")

            load_chunk(0)
            prep_chunk(0)
            for ci in range(nch):
                def chainE(hg, ci=ci):
                    mt_c = mt_d[hg % 2]
                    acs_c = acs_d[hg % 2]
                    etile = et_d2[hg % 2]
                    r.add("sp_q", (lambda e, ci=ci, hg=hg, acs_c=acs_c:
                                   e.dma_start(
                        out=acs_c.ap,
                        in_=io["acs_scr"][ci][hg * 16:(hg + 1) * 16, :]
                        .rearrange("h l -> (h l)").unsqueeze(0)
                        .broadcast_to((128, 2048)))),
                        [f"acs_scr{ci}"], acs_c.k, f"t1ld{ci}.{hg}")
                    r.add("pl", (lambda e, ci=ci, hg=hg, acs_c=acs_c,
                                 etile=etile: e.tensor_tensor(
                        out=etile.ap.rearrange("p (h l) -> p h l", l=128),
                        in0=acs_c.ap.rearrange("p (h l) -> p h l", l=128),
                        in1=bcast(a_csT.ap[:, ci * 128 + hg * 16:
                                           ci * 128 + (hg + 1) * 16], 16, 128),
                        op=ALU.subtract)),
                        acs_c.k + a_csT.k, etile.k, f"esub{ci}.{hg}")
                    r.add("act", (lambda e, etile=etile: e.activation(
                        out=etile.ap, in_=etile.ap, func=AF.Relu, scale=-1.0)),
                        etile.k, etile.k, f"eclamp{ci}.{hg}")
                    r.add("act", (lambda e, etile=etile: e.activation(
                        out=etile.ap, in_=etile.ap, func=AF.Exp, scale=-1.0)),
                        etile.k, etile.k, f"eexp{ci}.{hg}")
                    r.add("dve", (lambda e, hg=hg, mt_c=mt_c, etile=etile:
                                  e.tensor_tensor(
                        out=mt_c.ap.rearrange("p (r g l) -> p r g l", g=G, l=128),
                        in0=etile.ap.rearrange("p (r g l) -> p r g l", g=G, l=128),
                        in1=gm_all.ap.rearrange("p (g l) -> p g l", l=128)
                        .unsqueeze(1).broadcast_to((128, 2, G, 128)),
                        op=ALU.mult)),
                        etile.k + gm_all.k, mt_c.k, f"emul{ci}.{hg}")

                def mmhalf(hh, ci=ci):
                    hg, half = hh // 2, hh % 2
                    mt_c = mt_d[hg % 2]
                    bY = pA[2 * (hh % 2)]
                    bO = pA[2 * (hh % 2) + 1]

                    def ymm(e, hg=hg, half=half, mt_c=mt_c, bY=bY):
                        for j in range(8):
                            h = hg * 16 + half * 8 + j
                            jj = half * 8 + j
                            ins = e.matmul(bY.ap[:, j * 64:(j + 1) * 64],
                                           mt_c[:, jj * 128:(jj + 1) * 128],
                                           x_ch[:, h * 64:(h + 1) * 64],
                                           start=(j == 0), stop=False)
                        for j in range(8):
                            h = hg * 16 + half * 8 + j
                            ins = e.matmul(bY.ap[:, j * 64:(j + 1) * 64],
                                           ident_bf.ap,
                                           hsT2[:, h * 64:(h + 1) * 64],
                                           start=False, stop=(j == 7))
                        return ins
                    r.add("pe", ymm, mt_c.k + x_ch.k + hsT2.k + ident_bf.k,
                          bY.k, f"ydmm{ci}.{hh}")

                    def omm(e, hg=hg, half=half, bO=bO):
                        for j in range(8):
                            h = hg * 16 + half * 8 + j
                            ins = e.matmul(bO.ap[:, j * 64:(j + 1) * 64],
                                           bc2[:, (G + h % G) * 128:
                                               (G + h % G + 1) * 128],
                                           r_c[:, h * 64:(h + 1) * 64],
                                           start=(j == 0), stop=(j == 7))
                        return ins
                    r.add("pe", omm, bc2.k + r_c.k, bO.k, f"yomm{ci}.{hh}")

                def evhalf(hh, ci=ci):
                    hg, half = hh // 2, hh % 2
                    h0 = hg * 16 + half * 8
                    bY = pA[2 * (hh % 2)]
                    bO = pA[2 * (hh % 2) + 1]
                    y_half = y_half_d[hh % 2]
                    r.add("dve", (lambda e, h0=h0, bO=bO, y_half=y_half:
                                  e.tensor_tensor(
                        out=y_half.ap.rearrange("p (h q) -> p h q", q=PDIM),
                        in0=bO.ap.rearrange("p (h q) -> p h q", q=PDIM),
                        in1=bcast(expacsT.ap[:, h0:h0 + 8], 8, PDIM),
                        op=ALU.mult)),
                        bO.k + expacsT.k, y_half.k, f"yo{ci}.{h0}")
                    r.add("dve", (lambda e, bY=bY, y_half=y_half:
                                  e.tensor_tensor(
                        out=y_half.ap, in0=y_half.ap, in1=bY.ap, op=ALU.add)),
                        y_half.k + bY.k, y_half.k, f"ya{ci}.{h0}")
                    for j in range(4):
                        r.add("pe", (lambda e, j=j, half=half, y_half=y_half:
                                     e.transpose(
                            pB[half].ap[:, j * 128:(j + 1) * 128],
                            y_half[:, j * 128:(j + 1) * 128], ident_f.ap)),
                            y_half.k + ident_f.k, pB[half].k,
                            f"ytr{ci}.{h0}.{j}")
                    blk0 = (h0 * 64) // 128
                    r.add("dve", (lambda e, half=half, blk0=blk0:
                                  e.tensor_tensor(
                        out=xg[:, blk0 * 128:(blk0 + 4) * 128],
                        in0=pB[half].ap,
                        in1=gate_ch[:, blk0 * 128:(blk0 + 4) * 128],
                        op=ALU.mult)),
                        pB[half].k + gate_ch.k, xg.k, f"xg{ci}.{h0}")

                if ci > 0:
                    prep_chunk(ci)
                chainE(0)
                for hh in range(16):
                    if hh % 2 == 0 and hh // 2 + 1 < 8:
                        chainE(hh // 2 + 1)
                    mmhalf(hh)
                    if hh >= 1:
                        evhalf(hh - 1)
                evhalf(15)
                if ci + 1 < nch:
                    load_chunk(ci + 1)
                if cfg.p2b_cut < 60:
                    continue
                for g in range(G):
                    r.add("act", (lambda e, g=g: e.activation(
                        out=sq_st.ap, in_=xg[:, g * 1024:(g + 1) * 1024],
                        func=AF.Square)), xg.k, sq_st.k, f"sq{ci}.{g}")

                    def ssum(e, g=g):
                        bank = pB[g // 4]
                        for pb in range(8):
                            ins = e.matmul(
                                bank.ap[0:1, (g % 4) * 128:(g % 4 + 1) * 128],
                                ones_col.ap,
                                sq_st[:, pb * 128:(pb + 1) * 128],
                                start=(pb == 0), stop=(pb == 7))
                        return ins
                    r.add("pe", ssum, sq_st.k + ones_col.k, pB[g // 4].k,
                          f"ss{ci}.{g}")
                for j in range(2):
                    r.add("act", (lambda e, j=j: e.copy(
                        out=rstd.ap[0:1, j * 512:(j + 1) * 512],
                        in_=pB[j].ap[0:1, 0:512])),
                        pB[j].k, rstd.k, f"rse{ci}.{j}")
                r.add("dve", (lambda e: e.tensor_scalar(
                    out=rstd.ap, in0=rstd.ap, scalar1=1.0 / 1024.0, scalar2=EPS,
                    op0=ALU.mult, op1=ALU.add)),
                    rstd.k, rstd.k, f"rs1{ci}")
                r.add("act", (lambda e: e.activation(out=rstd.ap, in_=rstd.ap,
                                                     func=AF.Sqrt)),
                      rstd.k, rstd.k, f"rs2{ci}")
                r.add("dve", (lambda e: e.reciprocal(out=rstd.ap, in_=rstd.ap)),
                      rstd.k, rstd.k, f"rs3{ci}")
                for g in range(G):
                    bank = pB[g % 2]
                    r.add("pe", (lambda e, g=g, bank=bank: e.matmul(
                        bank.ap[:, 0:128], ones_row.ap,
                        rstd.ap[0:1, g * 128:(g + 1) * 128],
                        start=True, stop=True)),
                        ones_row.k + rstd.k, bank.k, f"rbc{ci}.{g}")
                    r.add("dve", (lambda e, g=g, bank=bank: e.tensor_copy(
                        out=rbc.ap[:, g * 128:(g + 1) * 128],
                        in_=bank.ap[:, 0:128])),
                        bank.k, rbc.k, f"rbce{ci}.{g}")
                for g in range(G):
                    r.add("dve", (lambda e, g=g: e.tensor_tensor(
                        out=normed_st[:, g * 1024:(g + 1) * 1024].rearrange(
                            "p (b t) -> p b t", t=128),
                        in0=xg[:, g * 1024:(g + 1) * 1024].rearrange(
                            "p (b t) -> p b t", t=128),
                        in1=rbc.ap[:, g * 128:(g + 1) * 128].unsqueeze(1)
                        .broadcast_to((128, 8, 128)),
                        op=ALU.mult)),
                        xg.k + rbc.k, normed_st.k, f"nm1{ci}.{g}")
                    r.add("dve", (lambda e, g=g: e.tensor_tensor(
                        out=normed_st[:, g * 1024:(g + 1) * 1024].rearrange(
                            "p (b t) -> p b t", t=128),
                        in0=normed_st[:, g * 1024:(g + 1) * 1024].rearrange(
                            "p (b t) -> p b t", t=128),
                        in1=bcast(nw_sb.ap[:, g * 8:(g + 1) * 8], 8, 128),
                        op=ALU.mult)),
                        normed_st.k + nw_sb.k, normed_st.k, f"nm2{ci}.{g}")
                for q in range(4):
                    r.add("sp_q", (lambda e, ci=ci, q=q: e.dma_start(
                        out=io["normed_scr"].rearrange("(b p) t -> p b t", p=128)
                        [:, q * 16:(q + 1) * 16, ci * CHK:(ci + 1) * CHK],
                        in_=normed_st[:, q * 2048:(q + 1) * 2048].rearrange(
                            "p (b t) -> p b t", t=CHK))),
                        normed_st.k, [f"normed_scr{ci}"], f"nw{ci}.{q}")

        if cfg.past("p3"):
            nstrip3 = t_own // 512
            np3 = RA.carve(0, [128, 64 * 512], BF16)
            out3 = [RA.carve(65536 + i * 16384, [128, HID], F32) for i in range(4)]
            w_p3 = [RA.carve(131072 + i * 4096, [128, 4 * 512], BF16)
                    for i in range(2)]
            xres = RB.carve(0, [128, HID], F32)
            ln_bf = RB.carve(16384, [128, HID], BF16)
            lnfm_st = RB.carve(24576, [128, NKB * 128], BF16)

            for s3 in range(nstrip3):
                tok0 = s3 * 512
                for q in range(8):
                    r.add("sp_q" if q % 2 == 0 else "act_q",
                          (lambda e, tok0=tok0, q=q: e.dma_start(
                        out=np3[:, q * 8 * 512:(q + 1) * 8 * 512].rearrange(
                            "p (b t) -> p b t", t=512),
                        in_=io["normed_scr"].rearrange("(b p) t -> p b t", p=128)
                        [:, q * 8:(q + 1) * 8, tok0:tok0 + 512])),
                        [f"normed_scr{i}" for i in range(tok0 // CHK,
                                                         tok0 // CHK + 4)],
                        np3.sub(q * 8192, (q + 1) * 8192), f"npld{s3}.{q}")
                for og in range(8):
                    for cq in range(16):
                        slot = cq % 2
                        r.add("act_q" if cq % 2 == 0 else "sp_q",
                              (lambda e, og=og, cq=cq, slot=slot:
                                        e.dma_start(
                            out=w_p3[slot].ap.rearrange("p (c o) -> p c o", o=512),
                            in_=io["wout_bf"][cq * 4:(cq + 1) * 4, :,
                                              og * 512:(og + 1) * 512]
                            .rearrange("c p o -> p c o"))),
                            [f"wout_bf{(cq * 4) // 8 * 8}"], w_p3[slot].k,
                            f"wp3{s3}.{og}.{cq}")

                        def p3mm(e, og=og, cq=cq, slot=slot):
                            for c4 in range(4):
                                chb = cq * 4 + c4
                                for tb in range(4):
                                    ins = e.matmul(
                                        pA[tb].ap,
                                        np3[:, chb * 512 + tb * 128:
                                            chb * 512 + (tb + 1) * 128],
                                        w_p3[slot][:, c4 * 512:(c4 + 1) * 512],
                                        start=(cq == 0 and c4 == 0),
                                        stop=(cq == 15 and c4 == 3))
                            return ins
                        r.add("pe", p3mm,
                              np3.sub(cq * 4096, (cq + 1) * 4096)
                              + w_p3[slot].k,
                              [k for tb in range(4) for k in pA[tb].k],
                              f"p3mm{s3}.{og}.{cq}")
                    for tb in range(4):
                        r.add("act", (lambda e, og=og, tb=tb: e.copy(
                            out=out3[tb].ap[:, og * 512:(og + 1) * 512],
                            in_=pA[tb].ap)),
                            pA[tb].k, out3[tb].k, f"p3ev{s3}.{og}.{tb}")
                for tb in range(4):
                    trow = tok0 + tb * 128
                    r.add("sp_q", (lambda e, trow=trow: e.dma_start(
                        out=xres.ap, in_=io["x_pad"][halo + trow:
                                                     halo + trow + 128, :])),
                        [], xres.k, f"xres{s3}.{tb}")
                    emit_ln(r, cfg, out3[tb], xres, stats, io, trow,
                            "ln1tm_scr", f"ln1tm{trow // 128}", f"l1_{trow}",
                            ident_bf, ln_bf, lnfm_st, pT, io_fm="ln1fm_scr",
                            fm_key=f"ln1fm{trow // 128}")

        if cfg.past("p4"):
            nstrip4 = t_own // 512
            xln4 = RA.carve(0, [128, NKB * 512], BF16)
            h4 = RA.carve(32768, [128, 32 * 512], BF16)
            w12sb = [RA.carve(65536 + i * 16384, [128, NKB * 256], BF16)
                     for i in range(2)]
            b3bc = RA.carve(98304, [128, HID], F32)
            out4 = [RA.carve(114688, [128, HID], F32),
                    RA.carve(81920, [128, HID], F32)]
            b3st = RA.carve(114688, [1, HID], F32)
            w3sb = [RA.carve(131072 + i * 4096, [128, 4 * 512], BF16)
                    for i in range(2)]
            lnres = RB.carve(0, [128, HID], F32)
            h1s = RB.carve(16384, [128, 512], BF16)
            h2t = RB.carve(17408, [128, 512], F32)

            r.add("sp_q", lambda e: e.dma_start(out=b3st.ap, in_=io["b3_row"][:]),
                  [], b3st.k, "b3ld")
            for j in range(8):
                r.add("pe", (lambda e, j=j: e.matmul(
                    pB[j % 2].ap, ones_row.ap,
                    b3st.ap[0:1, j * 512:(j + 1) * 512], start=True, stop=True)),
                    ones_row.k + b3st.k, pB[j % 2].k, f"b3mm{j}")
                r.add("dve", (lambda e, j=j: e.tensor_copy(
                    out=b3bc.ap[:, j * 512:(j + 1) * 512], in_=pB[j % 2].ap)),
                    pB[j % 2].k, b3bc.k, f"b3ev{j}")

            for s4 in range(nstrip4):
                tok0 = s4 * 512
                for q in range(2):
                    r.add("sp_q", (lambda e, tok0=tok0, q=q: e.dma_start(
                        out=xln4[:, q * 16 * 512:(q + 1) * 16 * 512].rearrange(
                            "p (b t) -> p b t", t=512),
                        in_=io["ln1fm_scr"].rearrange("(b p) t -> p b t", p=128)
                        [:, q * 16:(q + 1) * 16, tok0:tok0 + 512])),
                        [f"ln1fm{i}" for i in range(tok0 // 128, tok0 // 128 + 4)],
                        xln4.sub(q * 16384, (q + 1) * 16384), f"xlnld{s4}.{q}")
                for pr in range(32):
                    slot = pr % 2
                    for q in range(2):
                        r.add("act_q" if q == 0 else "sp_q",
                              (lambda e, pr=pr, slot=slot, q=q:
                                        e.dma_start(
                            out=w12sb[slot][:, q * 16 * 256:(q + 1) * 16 * 256]
                            .rearrange("p (c w) -> p c w", w=256),
                            in_=io["w12_bf"][pr][q * 16:(q + 1) * 16]
                            .rearrange("c p w -> p c w"))),
                            [f"w12_bf{pr // 4 * 4}"], w12sb[slot].k,
                            f"w12ld{s4}.{pr}.{q}")
                    for half in range(2):
                        def w12mm(e, pr=pr, slot=slot, half=half):
                            for chb in range(NKB):
                                ins = e.matmul(
                                    pA[half].ap,
                                    w12sb[slot][:, chb * 256 + half * 128:
                                                chb * 256 + half * 128 + 128],
                                    xln4[:, chb * 512:(chb + 1) * 512],
                                    start=(chb == 0), stop=(chb == NKB - 1))
                            return ins
                        r.add("pe", w12mm, w12sb[slot].k + xln4.k, pA[half].k,
                              f"w12mm{s4}.{pr}.{half}")
                    r.add("act", (lambda e, pr=pr: e.activation(
                        out=h1s.ap, in_=pA[0].ap, func=AF.Silu,
                        bias=b12_sb.ap[:, pr:pr + 1], scale=1.0)),
                        pA[0].k + b12_sb.k, h1s.k, f"h1s{s4}.{pr}")
                    r.add("dve", (lambda e, pr=pr: e.tensor_scalar(
                        out=h2t.ap, in0=pA[1].ap,
                        scalar1=b12_sb.ap[:, 32 + pr:33 + pr], scalar2=None,
                        op0=ALU.add)),
                        pA[1].k + b12_sb.k, h2t.k, f"h2{s4}.{pr}")
                    r.add("dve", (lambda e, pr=pr: e.tensor_tensor(
                        out=h4[:, pr * 512:(pr + 1) * 512], in0=h1s.ap,
                        in1=h2t.ap, op=ALU.mult)),
                        h1s.k + h2t.k, h4.k, f"h4{s4}.{pr}")
                for sub in range(2):
                    for og in range(8):
                        for mq in range(8):
                            slot = mq % 2
                            r.add("act_q" if mq % 2 == 0 else "sp_q",
                                  (lambda e, og=og, mq=mq, slot=slot:
                                            e.dma_start(
                                out=w3sb[slot].ap.rearrange("p (c o) -> p c o",
                                                            o=512),
                                in_=io["w3_bf"][mq * 4:(mq + 1) * 4, :,
                                                og * 512:(og + 1) * 512]
                                .rearrange("c p o -> p c o"))),
                                [f"w3_bf{(mq * 4) // 8 * 8}"], w3sb[slot].k,
                                f"w3ld{s4}.{sub}.{og}.{mq}")

                            def w3mm(e, og=og, mq=mq, slot=slot, sub=sub):
                                for m4 in range(4):
                                    mb = mq * 4 + m4
                                    for tb in range(2):
                                        ins = e.matmul(
                                            pA[2 + tb].ap,
                                            h4[:, mb * 512 + sub * 256 +
                                               tb * 128:
                                               mb * 512 + sub * 256 +
                                               (tb + 1) * 128],
                                            w3sb[slot][:, m4 * 512:
                                                       (m4 + 1) * 512],
                                            start=(mq == 0 and m4 == 0),
                                            stop=(mq == 7 and m4 == 3))
                                return ins
                            r.add("pe", w3mm, h4.k + w3sb[slot].k,
                                  pA[2].k + pA[3].k, f"w3mm{s4}.{sub}.{og}.{mq}")
                        for tb in range(2):
                            r.add("act", (lambda e, og=og, tb=tb: e.copy(
                                out=out4[tb].ap[:, og * 512:(og + 1) * 512],
                                in_=pA[2 + tb].ap)),
                                pA[2 + tb].k, out4[tb].k,
                                f"p4ev{s4}.{sub}.{og}.{tb}")
                    for tb in range(2):
                        trow = tok0 + sub * 256 + tb * 128
                        r.add("dve", (lambda e, tb=tb: e.tensor_tensor(
                            out=out4[tb].ap, in0=out4[tb].ap, in1=b3bc.ap,
                            op=ALU.add)),
                            out4[tb].k + b3bc.k, out4[tb].k, f"b3a{s4}.{sub}.{tb}")
                        r.add("sp_q", (lambda e, trow=trow: e.dma_start(
                            out=lnres.ap, in_=io["ln1tm_scr"][trow:trow + 128, :])),
                            [f"ln1tm{trow // 128}"], lnres.k, f"lrld{s4}.{trow}")
                        emit_ln(r, cfg, out4[tb], lnres, stats, io, trow,
                                "out", f"out{trow // 128}", f"l2_{trow}",
                                ident_bf, None, None, pT)

        fence_keys = []
        if cfg.past("p4"):
            fence_keys += [f"out{i}" for i in range(t_own // 128)]
        for name, dst in taps.items():
            src = io[name]
            nrow = src.shape[0]
            inner = int(np.prod(src.shape[1:])) if len(src.shape) > 1 else 1
            esz = 2 if src.dtype == BF16 else 4
            step = max(1, min(nrow, (1 << 22) // max(1, inner * esz)))
            keys = all_keys_of(name, cfg)
            i0 = 0
            while i0 < nrow:
                i1 = min(nrow, i0 + step)
                r.add("sp_q", (lambda e, name=name, dst=dst, i0=i0, i1=i1:
                               e.dma_start(out=dst[i0:i1], in_=io[name][i0:i1])),
                      keys, [f"tapw_{name}"], f"tap_{name}")
                i0 = i1
            fence_keys.append(f"tapw_{name}")
        r.add("pl", lambda e: e.nop(), fence_keys, [], "fence")

        stats_ = r.assign_and_emit(nc, block, st)
    return nc, stats_


def emit_ln(r, cfg, src, res, stats, io, trow, out_name, out_key, tag,
            ident_bf, ln_bf, lnfm_st, pT, io_fm=None, fm_key=None):
    ALU_ = ALU
    r.add("dve", (lambda e: e.tensor_tensor(out=src.ap, in0=src.ap, in1=res.ap,
                                            op=ALU_.add)),
          src.k + res.k, src.k, f"{tag}_res")
    r.add("act", (lambda e: e.activation(out=res.ap, in_=src.ap,
                                         func=AF.Identity,
                                         accum_out=stats.ap[:, 0:1])),
          src.k, res.k + stats.k, f"{tag}_sum")
    r.add("act", (lambda e: e.activation(out=res.ap, in_=src.ap, func=AF.Square,
                                         accum_out=stats.ap[:, 1:2])),
          src.k, res.k + stats.k, f"{tag}_sq")
    r.add("dve", (lambda e: e.tensor_scalar(out=stats.ap[:, 0:1],
                                            in0=stats.ap[:, 0:1],
                                            scalar1=1.0 / HID, scalar2=None,
                                            op0=ALU_.mult)),
          stats.k, stats.k, f"{tag}_s1")
    r.add("dve", (lambda e: e.tensor_tensor(out=stats.ap[:, 2:3],
                                            in0=stats.ap[:, 0:1],
                                            in1=stats.ap[:, 0:1],
                                            op=ALU_.mult)),
          stats.k, stats.k, f"{tag}_s2")
    r.add("dve", (lambda e: e.tensor_scalar(out=stats.ap[:, 1:2],
                                            in0=stats.ap[:, 1:2],
                                            scalar1=1.0 / HID, scalar2=None,
                                            op0=ALU_.mult)),
          stats.k, stats.k, f"{tag}_s3")
    r.add("dve", (lambda e: e.tensor_tensor(out=stats.ap[:, 3:4],
                                            in0=stats.ap[:, 1:2],
                                            in1=stats.ap[:, 2:3],
                                            op=ALU_.subtract)),
          stats.k, stats.k, f"{tag}_s4")
    r.add("dve", (lambda e: e.tensor_scalar(out=stats.ap[:, 3:4],
                                            in0=stats.ap[:, 3:4],
                                            scalar1=EPS, scalar2=None,
                                            op0=ALU_.add)),
          stats.k, stats.k, f"{tag}_s4b")
    r.add("act", (lambda e: e.activation(out=stats.ap[:, 4:5],
                                         in_=stats.ap[:, 3:4], func=AF.Sqrt)),
          stats.k, stats.k, f"{tag}_s5")
    r.add("dve", (lambda e: e.reciprocal(out=stats.ap[:, 5:6],
                                         in_=stats.ap[:, 4:5])),
          stats.k, stats.k, f"{tag}_s6")
    r.add("dve", (lambda e: e.tensor_scalar(out=stats.ap[:, 6:7],
                                            in0=stats.ap[:, 0:1],
                                            scalar1=-1.0, scalar2=None,
                                            op0=ALU_.mult)),
          stats.k, stats.k, f"{tag}_s7")
    r.add("dve", (lambda e: e.tensor_scalar(out=src.ap, in0=src.ap,
                                            scalar1=stats.ap[:, 6:7],
                                            scalar2=stats.ap[:, 5:6],
                                            op0=ALU_.add, op1=ALU_.mult)),
          src.k + stats.k, src.k, f"{tag}_nrm")
    r.add("sp_q", (lambda e: e.dma_start(out=io[out_name][trow:trow + 128, :],
                                         in_=src.ap)),
          src.k, [out_key], f"{tag}_w")
    if io_fm is not None:
        r.add("dve", (lambda e: e.tensor_copy(out=ln_bf.ap, in_=src.ap)),
              src.k, ln_bf.k, f"{tag}_bf")
        for cb in range(NKB):
            bank = pT[cb % 2]
            r.add("pe", (lambda e, cb=cb, bank=bank: e.transpose(
                bank.ap, ln_bf[:, cb * 128:(cb + 1) * 128], ident_bf.ap)),
                ln_bf.k + ident_bf.k, bank.k, f"{tag}_tr{cb}")
            r.add("dve", (lambda e, cb=cb, bank=bank: e.tensor_copy(
                out=lnfm_st.ap[:, cb * 128:(cb + 1) * 128], in_=bank.ap)),
                bank.k, lnfm_st.k, f"{tag}_tre{cb}")
        for q in range(2):
            r.add("sp_q", (lambda e, q=q: e.dma_start(
                out=io[io_fm].rearrange("(b p) t -> p b t", p=128)
                [:, q * 16:(q + 1) * 16, trow:trow + 128],
                in_=lnfm_st.ap[:, q * 16 * 128:(q + 1) * 16 * 128].rearrange(
                    "p (b t) -> p b t", t=128))),
                lnfm_st.k, [fm_key], f"{tag}_fmw{q}")


def all_keys_of(name, cfg):
    table = {
        "gate_scr": [f"gate_scr{i}" for i in range(8)],
        "conv_scr": [f"conv_scr{i}" for i in range(10)],
        "acs_scr": [f"acs_scr{i}" for i in range(cfg.nch)],
        "hst_scr": [f"hst_scr{i}" for i in range(cfg.nch)],
        "nloc_scr": [f"nloc_scr{i}" for i in range(cfg.nch)],
        "normed_scr": [f"normed_scr{i}" for i in range(cfg.nch)],
        "ln1tm_scr": [f"ln1tm{i}" for i in range(cfg.t_own // 128)],
        "ln1fm_scr": [f"ln1fm{i}" for i in range(cfg.t_own // 128)],
        "s_in_scr": ["s_in_scr"],
        "snd": ["snd"],
        "gth": ["gth"],
        "x_bf": ["x_bf"],
        "tcg_scr": ["tcg0", "tcg1"],
    }
    return table.get(name, [name])


from concourse.bass_utils import run_bass_kernel_spmd


def kernel(**inputs):
    cfg = Cfg(t_own=2048, halo=128, ncores=8, upto="p4", taps=())
    sh = host_shared(inputs)
    in_maps = [host_core_map(cfg, inputs, c, sh) for c in range(cfg.ncores)]
    nc, _ = build(cfg)
    res = run_bass_kernel_spmd(nc, in_maps, core_ids=list(range(cfg.ncores)))
    x = np.asarray(inputs["L0_attn_hidden_states"])
    B, L = x.shape[0], x.shape[1]
    out = np.empty((B, L, HID), np.float32)
    for c in range(cfg.ncores):
        seq, part = c % B, c // B
        out[seq, part * cfg.t_own:(part + 1) * cfg.t_own] = res.results[c]["out"]
    return out


# version 13 (final)
# speedup vs baseline: 1.0924x; 1.0924x over previous
import contextlib

COMPUTE = ("pe", "dve", "act", "pl")
SEQ_OF = {"pe": "tensor", "dve": "vector", "act": "scalar", "act_q": "scalar",
          "pl": "gpsimd", "pl_q": "gpsimd", "pl_cc": "gpsimd", "sp_q": "sync"}
DMA_POOL = 10


class Op:
    __slots__ = ("eng", "emit", "reads", "writes", "idx", "need_inc", "waits",
                 "sem", "inc_val", "tag", "seq")

    def __init__(self, eng, emit, reads, writes, tag):
        self.eng = eng
        self.emit = emit
        self.reads = reads
        self.writes = writes
        self.tag = tag
        self.need_inc = False
        self.waits = []


class Rec:
    def __init__(self):
        self.ops = []

    def add(self, eng, emit, reads=(), writes=(), tag=""):
        op = Op(eng, emit, list(reads), list(writes), tag)
        op.idx = len(self.ops)
        self.ops.append(op)
        return op

    def resolve(self):
        last_writer = {}
        readers = {}
        edges = {}
        for op in self.ops:
            deps = set()
            for r in op.reads:
                w = last_writer.get(r)
                if w is not None:
                    deps.add(w)
            for w_ in op.writes:
                w = last_writer.get(w_)
                if w is not None:
                    deps.add(w)
                for rd in readers.get(w_, ()):
                    deps.add(rd)
            deps.discard(op)
            edges[op.idx] = deps
            for r in op.reads:
                readers.setdefault(r, []).append(op)
            for w_ in op.writes:
                last_writer[w_] = op
                readers[w_] = []
        return edges

    def assign_and_emit(self, nc, block, st, verbose=False):
        edges = self.resolve()

        drain_before = set()
        for op in self.ops:
            for p in edges[op.idx]:
                same_stream = p.eng == op.eng
                if same_stream and op.eng in COMPUTE:
                    if op.eng != "pe":
                        drain_before.add((op.idx, op.eng))
                else:
                    p.need_inc = True

        sems = {}
        for e in COMPUTE:
            sems[e] = (st.enter_context(nc.semaphore(f"c_{e}")), )
        pool = {}
        for q in ("sp_q", "act_q", "pl_q"):
            pool[q] = [st.enter_context(nc.semaphore(f"{q}{i}"))
                       for i in range(DMA_POOL)]
        cc_sem = st.enter_context(nc.semaphore("cc_s"))

        cnt = {e: 0 for e in COMPUTE}
        pool_idx = {q: 0 for q in pool}
        pool_cnt = {}
        pool_last_op = {}
        cc_cnt = 0
        for op in self.ops:
            if op.eng in COMPUTE:
                if op.need_inc:
                    cnt[op.eng] += 1
                    op.sem = sems[op.eng][0]
                    op.inc_val = cnt[op.eng]
                else:
                    op.sem = None
            elif op.eng == "pl_cc":
                cc_cnt += 1
                op.sem = cc_sem
                op.inc_val = cc_cnt
                op.need_inc = True
            else:
                q = op.eng
                s = pool[q][pool_idx[q] % DMA_POOL]
                pool_idx[q] += 1
                prev = pool_last_op.get(id(s))
                if prev is not None:
                    op.waits.append((s, pool_cnt[id(s)]))
                pool_cnt[id(s)] = pool_cnt.get(id(s), 0) + 16
                pool_last_op[id(s)] = op
                op.sem = s
                op.inc_val = pool_cnt[id(s)]
                op.need_inc = True

        for op in self.ops:
            for p in sorted(edges[op.idx], key=lambda o: o.idx):
                same_stream = p.eng == op.eng
                if same_stream and op.eng in COMPUTE:
                    continue
                assert p.sem is not None, (p.tag, op.tag)
                op.waits.append((p.sem, p.inc_val))

        self.resolved_edges = edges
        by_seq = {}
        for op in self.ops:
            by_seq.setdefault(SEQ_OF[op.eng], []).append(op)

        stats = {"ops": len(self.ops), "incs": 0, "waits": 0, "drains": 0}
        for seq_name, ops in by_seq.items():
            eng = getattr(nc, {"tensor": "tensor", "vector": "vector",
                               "scalar": "scalar", "gpsimd": "gpsimd",
                               "sync": "sync"}[seq_name])
            deco = getattr(block, seq_name)

            def emit_all(e, ops=ops, seq_name=seq_name):
                waited = {}
                drained_at = -1
                prev_idx = -1
                for op in ops:
                    if (op.idx, op.eng) in drain_before:
                        prods = [p for p in self.resolved_edges[op.idx]
                                 if p.eng == op.eng]
                        if any(p.idx > drained_at for p in prods):
                            e.drain()
                            drained_at = prev_idx
                            stats["drains"] += 1
                    for s, v in op.waits:
                        if waited.get(id(s), -1) >= v:
                            continue
                        e.wait_ge(s, v)
                        waited[id(s)] = v
                        stats["waits"] += 1
                    instr = op.emit(e)
                    if op.need_inc:
                        k = 16 if op.eng in ("sp_q", "act_q", "pl_q") else 1
                        instr.then_inc(op.sem, k)
                        stats["incs"] += 1
                    prev_idx = op.idx

            deco(emit_all)
        return stats


"""Megakernel builder for mk_seq_postln_d1 (mamba2 mixer + postln + SwiGLU FFN).

Token-sharded: each core owns `t_own` tokens of one sequence (half-sequence in
the full config) plus a `halo`-token frame prefix (zeros at sequence start).
Sync edges are resolved by the recorder in rec.py.

Phase flow per core:
  setup -> P1 (in_proj + conv + dt) -> P1b (dt/A_cs prep) -> P2a (states)
  -> EX (AllGather + peer pick) -> P2b (Y + gating + RMS) -> P3 (out_proj+LN1)
  -> P4 (FFN + LN2) -> out

SBUF: two byte regions RA (136K) / RB (56K) aliased across phases + small
static tiles; dependency keys at 2KB granularity keep aliasing safe.
"""
import contextlib
import numpy as np
import concourse.bass as bass
import concourse.mybir as mybir

F32 = mybir.dt.float32
BF16 = mybir.dt.bfloat16
I32 = mybir.dt.int32
U8 = mybir.dt.uint8
AF = mybir.ActivationFunctionType
ALU = mybir.AluOpType

HID = 4096
H = 128
PDIM = 64
ND = 128
G = 8
KCONV = 4
CHK = 128
INTER = 8192
CONVD = 10240
PROJ = 18560
EPS = 1e-5
NFB = PROJ // 128
NKB = HID // 128
NEG = -1e30
BLK = 2048

RA_SZ = 139264
CAST_SLICES = (
    [("wout_bf", "wout_t", i * 8, (i + 1) * 8) for i in range(8)]
    + [("w12_bf", "w12_t", i * 4, (i + 1) * 4) for i in range(8)]
    + [("w3_bf", "w3_t", i * 8, (i + 1) * 8) for i in range(4)]
)
RB_SZ = 57344


class Cfg:
    def __init__(self, t_own=2048, halo=128, ncores=8, upto="p4", taps=(),
                 p2b_cut=99):
        self.p2b_cut = p2b_cut
        self.t_own = t_own
        self.halo = halo
        self.tf = t_own + halo
        self.nch = t_own // CHK
        self.ncores = ncores
        self.upto = upto
        self.taps = tuple(taps)
        self.groups = []
        off = 0
        while off < self.tf:
            g = min(512, self.tf - off)
            self.groups.append((off, g))
            off += g
        self.groups_own = []
        off = self.halo
        while off < self.tf:
            g = min(512, self.tf - off)
            self.groups_own.append((off, g))
            off += g
        self.groups_hbc = []
        off = self.halo - 3
        while off < self.tf:
            g = min(512, self.tf - off)
            self.groups_hbc.append((off, g))
            off += g

    def past(self, phase):
        order = ["p1", "p2a", "ex", "p2b", "p3", "p4"]
        return order.index(self.upto) >= order.index(phase)


def host_shared(inputs):
    sh = {}
    w = np.asarray(inputs["L0_attn_in_proj_w"])
    sh["win_t"] = np.ascontiguousarray(
        w.reshape(NFB, 128, NKB, 128).transpose(0, 3, 2, 1))
    sh["wout_t"] = np.ascontiguousarray(
        np.asarray(inputs["L0_attn_out_proj_w"]).T.reshape(64, 128, HID))
    w12 = np.asarray(inputs["L0_ffn_w12"])
    sh["w12_t"] = np.ascontiguousarray(
        w12.T.reshape(NKB, 128, 2, 32, 128).transpose(3, 0, 1, 2, 4)
        .reshape(32, NKB, 128, 256))
    sh["w3_t"] = np.ascontiguousarray(
        np.asarray(inputs["L0_ffn_w3"]).T.reshape(32, 128, HID))
    cw = np.asarray(inputs["L0_attn_conv_w"]).reshape(CONVD, KCONV)
    sh["conv_w2"] = np.ascontiguousarray(
        cw.reshape(80, 128, KCONV).transpose(1, 0, 2).reshape(128, 80 * KCONV))
    sh["conv_b2"] = np.ascontiguousarray(
        np.asarray(inputs["L0_attn_conv_b"]).reshape(80, 128).T)
    sh["dt_bias_c"] = np.ascontiguousarray(
        np.asarray(inputs["L0_attn_dt_bias"]).reshape(128, 1))
    sh["a_log_c"] = np.ascontiguousarray(
        np.asarray(inputs["L0_attn_A_log"]).reshape(128, 1))
    sh["d_row"] = np.ascontiguousarray(np.asarray(inputs["L0_attn_D"]).reshape(1, 128))
    sh["nw_fm"] = np.ascontiguousarray(
        np.asarray(inputs["L0_attn_norm_weight"]).reshape(64, 128).T)
    sh["b12_col"] = np.ascontiguousarray(
        np.asarray(inputs["L0_ffn_b12"]).reshape(64, 128).T)
    sh["b3_row"] = np.ascontiguousarray(np.asarray(inputs["L0_ffn_b3"]).reshape(1, HID))
    return sh


def host_core_map(cfg, inputs, core_id, sh):
    x = np.asarray(inputs["L0_attn_hidden_states"])
    B = x.shape[0]
    seq = core_id % B
    part = core_id // B
    t0 = part * cfg.t_own
    x_pad = np.zeros((cfg.tf, HID), np.float32)
    x_pad[cfg.halo:] = x[seq, t0:t0 + cfg.t_own]
    if t0 > 0:
        x_pad[:cfg.halo] = x[seq, t0 - cfg.halo:t0]
    peer = core_id - B if part > 0 else core_id
    m = dict(sh)
    m["x_pad"] = x_pad
    m["idx_gather"] = (peer * 128 + np.arange(128, dtype=np.int32)).reshape(128, 1)
    m["sel"] = np.full((128, 1), 1.0 if part > 0 else 0.0, np.float32)
    return m


class T:
    def __init__(self, ap, keys, base=0, region=None):
        self.ap = ap
        self.k = keys
        self.base = base
        self.region = region

    def __getitem__(self, sl):
        return self.ap[sl]

    def sub(self, lo, hi):
        if self.region is None:
            return self.k
        b0 = (self.base + lo) // BLK
        b1 = (self.base + hi - 1) // BLK + 1
        return [f"{self.region}:{b}" for b in range(b0, b1)]


class Region:
    def __init__(self, st, nc, name, nbytes):
        self.name = name
        self.nbytes = nbytes
        self.t = st.enter_context(nc.sbuf_tensor(name, [128, nbytes], U8))

    def carve(self, off, shape, dt):
        esz = {F32: 4, BF16: 2, I32: 4}[dt]
        assert len(shape) == 2
        nb = shape[1] * esz
        assert off + nb <= self.nbytes, (self.name, off, nb, self.nbytes)
        ap = self.t[:, off:off + nb].bitcast(dt)
        if shape[0] != 128:
            ap = ap[0:shape[0], :]
        keys = [f"{self.name}:{b}" for b in range(off // BLK, (off + nb - 1) // BLK + 1)]
        return T(ap, keys, base=off, region=self.name)


def sbt(st, nc, name, shape, dt):
    t = st.enter_context(nc.sbuf_tensor(name, list(shape), dt))
    return T(t[:], [name])


def bcast(ap, h, p):
    return ap.unsqueeze(2).broadcast_to((ap.shape[0], h, p))


def build(cfg):
    nc = bass.Bass("TRN2", target_bir_lowering=False, debug=False)
    io = {}
    tf, t_own, halo, nch = cfg.tf, cfg.t_own, cfg.halo, cfg.nch

    def din(name, shape, dt):
        io[name] = nc.dram_tensor(name, list(shape), dt, kind="ExternalInput").ap()

    din("x_pad", [tf, HID], F32)
    din("win_t", [NFB, 128, NKB, 128], F32)
    din("wout_t", [64, 128, HID], F32)
    din("w12_t", [32, NKB, 128, 256], F32)
    din("w3_t", [32, 128, HID], F32)
    din("conv_w2", [128, 80 * KCONV], F32)
    din("conv_b2", [128, 80], F32)
    din("dt_bias_c", [128, 1], F32)
    din("a_log_c", [128, 1], F32)
    din("d_row", [1, 128], F32)
    din("nw_fm", [128, 64], F32)
    din("b12_col", [128, 64], F32)
    din("b3_row", [1, HID], F32)
    din("idx_gather", [128, 1], I32)
    din("sel", [128, 1], F32)
    io["out"] = nc.dram_tensor("out", [t_own, HID], F32, kind="ExternalOutput").ap()

    def scr(name, shape, dt):
        io[name] = nc.dram_tensor(name, list(shape), dt, kind="Internal").ap()

    scr("x_bf", [tf, HID], BF16)
    scr("gate_scr", [INTER, t_own], BF16)
    scr("conv_scr", [CONVD, t_own], BF16)
    scr("hst_scr", [nch, 128, INTER], BF16)
    scr("nloc_scr", [nch, 128, INTER], BF16)
    scr("acs_scr", [nch, 128, 128], F32)
    scr("tcg_scr", [2 * nch, 128], F32)
    scr("s_in_scr", [128, INTER], BF16)
    scr("normed_scr", [INTER, t_own], BF16)
    scr("ln1tm_scr", [t_own, HID], F32)
    scr("ln1fm_scr", [HID, t_own], BF16)
    scr("wout_bf", [64, 128, HID], BF16)
    scr("w12_bf", [32, NKB, 128, 256], BF16)
    scr("w3_bf", [32, 128, HID], BF16)
    scr("snd", [128, INTER], BF16)
    scr("gth", [cfg.ncores * 128, INTER], BF16)

    taps = {}
    for name in cfg.taps:
        src = io[name]
        taps[name] = nc.dram_tensor("tap_" + name, list(src.shape), src.dtype,
                                    kind="ExternalOutput").ap()

    r = Rec()
    with contextlib.ExitStack() as st:
        RA = Region(st, nc, "RA", RA_SZ)
        RB = Region(st, nc, "RB", RB_SZ)

        dt_t = sbt(st, nc, "dt_t", [128, tf], F32)
        ident_bf = sbt(st, nc, "ident_bf", [128, 128], BF16)
        ident_f = sbt(st, nc, "ident_f", [128, 128], F32)
        ones_row = sbt(st, nc, "ones_row", [1, 128], F32)
        ones_col = sbt(st, nc, "ones_col", [128, 1], F32)
        dbc = sbt(st, nc, "dbc", [128, 128], F32)
        nw_sb = sbt(st, nc, "nw_sb", [128, 64], F32)
        conv_w_sb = sbt(st, nc, "conv_w_sb", [128, 80 * KCONV], F32)
        conv_b_sb = sbt(st, nc, "conv_b_sb", [128, 80], F32)
        dt_bias_sb = sbt(st, nc, "dt_bias_sb", [128, 1], F32)
        a_col = sbt(st, nc, "a_col", [128, 1], F32)
        b12_sb = sbt(st, nc, "b12_sb", [128, 64], F32)
        idx_sb = sbt(st, nc, "idx_sb", [128, 1], I32)
        sel_sb = sbt(st, nc, "sel_sb", [128, 1], F32)
        tc_cols = sbt(st, nc, "tc_cols", [128, max(nch, 2)], F32)
        gc_cols = sbt(st, nc, "gc_cols", [128, max(nch, 2)], F32)
        stats = sbt(st, nc, "stats", [128, 16], F32)

        pA = [T(st.enter_context(nc.psum_tensor(f"pA{i}", [128, 512], F32))[:],
                [f"pA{i}"]) for i in range(4)]
        pB = [T(st.enter_context(nc.psum_tensor(f"pB{i}", [128, 512], F32))[:],
                [f"pB{i}"]) for i in range(2)]
        pT = [T(st.enter_context(nc.psum_tensor(f"pT{i}", [128, 128], BF16))[:],
                [f"pT{i}"]) for i in range(2)]

        block = st.enter_context(nc.Block())

        d_row_sb = RB.carve(50176, [1, 128], F32)
        row_st = RB.carve(50688, [1, 128], F32)
        tct = RB.carve(51200, [max(nch, 2), 128], F32)

        o = nch * 512
        a_csT = RA.carve(0, [128, nch * 128], F32)
        dtT = RA.carve(o, [128, nch * 128], F32)
        exp_tc = RA.carve(2 * o, [128, nch * 128], F32)
        exp_gc = RA.carve(3 * o, [128, nch * 128], F32)
        gm_all = RA.carve(4 * o, [128, G * 128], BF16)
        pers_end = 4 * o + G * 512

        for nm, tile in [("conv_w2", conv_w_sb), ("conv_b2", conv_b_sb),
                         ("dt_bias_c", dt_bias_sb), ("nw_fm", nw_sb),
                         ("b12_col", b12_sb), ("idx_gather", idx_sb),
                         ("sel", sel_sb), ("d_row", d_row_sb),
                         ("a_log_c", a_col)]:
            r.add("sp_q", (lambda e, nm=nm, tile=tile:
                           e.dma_start(out=tile.ap, in_=io[nm][:])),
                  [], tile.k, f"ld_{nm}")

        for q in range(4):
            r.add("pl_q", (lambda e, q=q: e.dma_start(
                out=io["x_bf"][:, q * 1024:(q + 1) * 1024],
                in_=io["x_pad"][:, q * 1024:(q + 1) * 1024])),
                [], [f"x_bf{q}"], f"cast_x{q}")

        for tile, val in [(ident_bf, 1.0), (ident_f, 1.0), (ones_row, 1.0),
                          (ones_col, 1.0)]:
            r.add("pl", (lambda e, tile=tile, val=val: e.memset(tile.ap, val)),
                  [], tile.k, "memset")
        for tile in (ident_bf, ident_f):
            r.add("pl", (lambda e, tile=tile: e.affine_select(
                out=tile.ap, in_=tile.ap, compare_op=ALU.is_equal, fill=0.0,
                base=0, pattern=[[-1, 128]], channel_multiplier=1)),
                tile.k, tile.k, "ident")

        r.add("pe", lambda e: e.matmul(pB[0][:, 0:128], ones_row.ap, d_row_sb.ap,
                                       start=True, stop=True),
              ones_row.k + d_row_sb.k, pB[0].k, "dbc_mm")
        r.add("dve", lambda e: e.tensor_copy(out=dbc.ap, in_=pB[0][:, 0:128]),
              pB[0].k, dbc.k, "dbc_ev")

        x_fm = RA.carve(0, [128, NKB * tf], BF16)
        wsb = [RB.carve(0, [128, NKB * 128], BF16),
               RB.carve(8192, [128, NKB * 128], BF16)]
        gate_stg = [RB.carve(16384, [128, tf], BF16),
                    RB.carve(16384 + 2 * tf, [128, tf], BF16)]
        conv_in = RB.carve(16384 + 4 * tf, [128, tf], F32)
        conv_acc = RB.carve(16384 + 8 * tf, [128, t_own], F32)
        hbc_bf = [RB.carve(16384 + 8 * tf + 4 * t_own, [128, t_own], BF16),
                  RB.carve(16384 + 8 * tf + 6 * t_own, [128, t_own], BF16)]

        for kb in range(NKB):
            key = x_fm.k[(kb * tf * 2) // BLK: ((kb + 1) * tf * 2 - 1) // BLK + 1]
            r.add("sp_q", (lambda e, kb=kb: e.dma_start(
                out=x_fm[:, kb * tf:(kb + 1) * tf],
                in_=io["x_bf"][:, kb * 128:(kb + 1) * 128], transpose=True)),
                [f"x_bf{kb // 8}"], key, f"xtr{kb}")

        sp_tmp = RB.carve(53760, [128, 512], F32)
        adt_t = RB.carve(55808, [128, 128], F32)
        acs_c2 = RB.carve(56320, [128, 128], F32)

        def emit_p1b_early(env):
            bia = dt_bias_sb.ap[:, 0:1]
            for off, gsz in cfg.groups:
                r.add("act", (lambda e, off=off, gsz=gsz: e.activation(
                    out=pB[0].ap[:, 0:gsz], in_=dt_t[:, off:off + gsz],
                    func=AF.Relu, bias=bia, scale=1.0)),
                    dt_t.k + dt_bias_sb.k, pB[0].k, f"sp_relu{off}")
                r.add("act", (lambda e, off=off, gsz=gsz: e.activation(
                    out=sp_tmp.ap[:, 0:gsz], in_=dt_t[:, off:off + gsz],
                    func=AF.Abs, bias=bia, scale=1.0)),
                    dt_t.k + dt_bias_sb.k, sp_tmp.k, f"sp_abs{off}")
                r.add("act", (lambda e, gsz=gsz: e.activation(
                    out=sp_tmp.ap[:, 0:gsz], in_=sp_tmp.ap[:, 0:gsz],
                    func=AF.Exp, scale=-1.0)), sp_tmp.k, sp_tmp.k, f"sp_e{off}")
                r.add("act", (lambda e, gsz=gsz: e.activation(
                    out=sp_tmp.ap[:, 0:gsz], in_=sp_tmp.ap[:, 0:gsz],
                    func=AF.Ln, bias=1.0)), sp_tmp.k, sp_tmp.k, f"sp_l{off}")
                r.add("dve", (lambda e, off=off, gsz=gsz: e.tensor_tensor(
                    out=dt_t[:, off:off + gsz], in0=pB[0].ap[:, 0:gsz],
                    in1=sp_tmp.ap[:, 0:gsz], op=ALU.add)),
                    pB[0].k + sp_tmp.k, dt_t.k, f"sp_a{off}")
            r.add("act", lambda e: e.activation(out=a_col.ap, in_=a_col.ap,
                                                func=AF.Exp),
                  a_col.k, a_col.k, "a_exp")
            r.add("dve", lambda e: e.tensor_scalar(out=a_col.ap, in0=a_col.ap,
                                                   scalar1=-1.0, scalar2=None,
                                                   op0=ALU.mult),
                  a_col.k, a_col.k, "a_neg")
            for ci in range(nch):
                sl = slice(halo + ci * CHK, halo + (ci + 1) * CHK)
                r.add("dve", (lambda e, sl=sl: e.tensor_scalar(
                    out=adt_t.ap, in0=dt_t[:, sl], scalar1=a_col.ap,
                    scalar2=None, op0=ALU.mult)),
                    dt_t.k + a_col.k, adt_t.k, f"adt{ci}")
                r.add("dve", (lambda e: e.tensor_tensor_scan(
                    out=acs_c2.ap, data0=adt_t.ap, data1=adt_t.ap,
                    initial=0.0, op0=ALU.add, op1=ALU.bypass)),
                    adt_t.k, acs_c2.k, f"acs{ci}")
                r.add("sp_q", (lambda e, ci=ci: e.dma_start(
                    out=io["acs_scr"][ci], in_=acs_c2.ap)),
                    acs_c2.k, [f"acs_scr{ci}"], f"acsd{ci}")
                r.add("dve", (lambda e, ci=ci: e.tensor_copy(
                    out=tc_cols.ap[:, ci:ci + 1], in_=acs_c2.ap[:, 127:128])),
                    acs_c2.k, tc_cols.k, f"tc{ci}")
            r.add("dve", lambda e: e.memset(gc_cols.ap, 0.0), [], gc_cols.k, "gc0")
            if nch > 1:
                r.add("dve", lambda e: e.tensor_tensor_scan(
                    out=gc_cols.ap[:, 1:nch], data0=tc_cols.ap[:, 0:nch - 1],
                    data1=tc_cols.ap[:, 0:nch - 1], initial=0.0, op0=ALU.add,
                    op1=ALU.bypass), tc_cols.k, gc_cols.k, "gcscan")
            for j, srci in enumerate([tc_cols, gc_cols]):
                r.add("pe", (lambda e, srci=srci: e.transpose(
                    pB[1].ap[0:nch, 0:128], srci.ap[:, 0:nch], ident_f.ap)),
                    srci.k + ident_f.k, pB[1].k, f"tr_tcg{j}")
                r.add("dve", (lambda e: e.tensor_copy(out=tct.ap[0:nch, :],
                                                      in_=pB[1].ap[0:nch, 0:128])),
                      pB[1].k, tct.k, f"tcte{j}")
                r.add("sp_q", (lambda e, j=j: e.dma_start(
                    out=io["tcg_scr"][j * nch:(j + 1) * nch, :],
                    in_=tct.ap[0:nch, :])),
                    tct.k, [f"tcg{j}"], f"tcgd{j}")

        fb_order = [NFB - 1] + list(range(NFB - 1))
        for fbi, fb in enumerate(fb_order):
            slot = fbi % 2
            r.add("pl_q", (lambda e, fb=fb, slot=slot: e.dma_start(
                out=wsb[slot].ap,
                in_=io["win_t"][fb].rearrange("k kb m -> k (kb m)"))),
                [], wsb[slot].k, f"wld{fb}")
            kind = "gate" if fb < 64 else ("hbc" if fb < NFB - 1 else "dt")
            fb_groups = (cfg.groups_own if kind == "gate" else
                         (cfg.groups_hbc if kind == "hbc" else cfg.groups))
            for gi, (off, gsz) in enumerate(fb_groups):
                bank = pA[(fbi * 8 + gi) % 4]

                def mm_group(e, slot=slot, off=off, gsz=gsz, bank=bank):
                    for kb in range(NKB):
                        ins = e.matmul(bank.ap[:, 0:gsz],
                                       wsb[slot][:, kb * 128:(kb + 1) * 128],
                                       x_fm[:, kb * tf + off:kb * tf + off + gsz],
                                       start=(kb == 0), stop=(kb == NKB - 1))
                    return ins
                r.add("pe", mm_group, wsb[slot].k + x_fm.k, bank.k, f"p1mm{fb}.{gi}")
                if kind == "gate":
                    dtile, dk = gate_stg[slot], gate_stg[slot].k
                elif kind == "hbc":
                    dtile, dk = conv_in, conv_in.k
                else:
                    dtile, dk = dt_t, dt_t.k
                r.add("dve", (lambda e, dtile=dtile, off=off, gsz=gsz, bank=bank:
                              e.tensor_copy(out=dtile.ap[:, off:off + gsz],
                                            in_=bank.ap[:, 0:gsz])),
                      bank.k, dk, f"p1ev{fb}.{gi}")
            if kind == "gate":
                r.add("sp_q", (lambda e, fb=fb, slot=slot: e.dma_start(
                    out=io["gate_scr"].rearrange("(b p) t -> p b t", p=128)[:, fb, :],
                    in_=gate_stg[slot][:, halo:tf])),
                    gate_stg[slot].k, [f"gate_scr{fb // 8}"], f"gate_w{fb}")
            if fbi == 0:
                emit_p1b_early(locals())
            if fb >= 30 and fb % 5 == 0 and (fb - 30) // 5 < len(CAST_SLICES):
                nm_d, nm_s, i0_, i1_ = CAST_SLICES[(fb - 30) // 5]
                if (nm_d != "wout_bf" or cfg.past("p3")) and \
                   (nm_d == "wout_bf" or cfg.past("p4")):
                    r.add("pl_q", (lambda e, nm_d=nm_d, nm_s=nm_s, i0_=i0_,
                                   i1_=i1_: e.dma_start(
                        out=io[nm_d][i0_:i1_], in_=io[nm_s][i0_:i1_])),
                        [], [f"{nm_d}{i0_}"], f"cast_{nm_d}{i0_}")
            if kind == "hbc":
                cblk = fb - 64
                hslot = cblk % 2
                r.add("dve", (lambda e, cblk=cblk: e.tensor_scalar(
                    out=conv_acc.ap, in0=conv_in[:, halo - 3:halo - 3 + t_own],
                    scalar1=conv_w_sb.ap[:, cblk * KCONV:cblk * KCONV + 1],
                    scalar2=None, op0=ALU.mult)),
                    conv_in.k + conv_w_sb.k, conv_acc.k, f"cv0_{cblk}")
                for d in range(1, KCONV):
                    r.add("dve", (lambda e, cblk=cblk, d=d: e.scalar_tensor_tensor(
                        out=conv_acc.ap,
                        in0=conv_in[:, halo - 3 + d:halo - 3 + d + t_own],
                        scalar=conv_w_sb.ap[:, cblk * KCONV + d:cblk * KCONV + d + 1],
                        in1=conv_acc.ap, op0=ALU.mult, op1=ALU.add)),
                        conv_in.k + conv_acc.k + conv_w_sb.k, conv_acc.k,
                        f"cv{d}_{cblk}")
                r.add("act", (lambda e, cblk=cblk, hslot=hslot: e.activation(
                    out=hbc_bf[hslot].ap, in_=conv_acc.ap, func=AF.Silu,
                    bias=conv_b_sb.ap[:, cblk:cblk + 1], scale=1.0)),
                    conv_acc.k + conv_b_sb.k, hbc_bf[hslot].k, f"silu{cblk}")
                r.add("sp_q", (lambda e, cblk=cblk, hslot=hslot: e.dma_start(
                    out=io["conv_scr"].rearrange("(b p) t -> p b t", p=128)[:, cblk, :],
                    in_=hbc_bf[hslot].ap)),
                    hbc_bf[hslot].k, [f"conv_scr{cblk // 8}"], f"conv_w{cblk}")

        for ci in range(nch):
            r.add("sp_q", (lambda e, ci=ci: e.dma_start(
                out=acs_c2.ap, in_=io["acs_scr"][ci])),
                [f"acs_scr{ci}"], acs_c2.k, f"acsl{ci}")
            r.add("pe", (lambda e, ci=ci: e.transpose(
                pB[0][:, 0:128], dt_t[:, halo + ci * CHK:halo + (ci + 1) * CHK],
                ident_f.ap)), dt_t.k + ident_f.k, pB[0].k, f"dtT{ci}")
            r.add("dve", (lambda e, ci=ci: e.tensor_copy(
                out=dtT.ap[:, ci * 128:(ci + 1) * 128], in_=pB[0][:, 0:128])),
                pB[0].k, dtT.k, f"dtTe{ci}")
            r.add("pe", (lambda e, ci=ci: e.transpose(
                pB[1][:, 0:128], acs_c2.ap, ident_f.ap)),
                acs_c2.k + ident_f.k, pB[1].k, f"acsT{ci}")
            r.add("dve", (lambda e, ci=ci: e.tensor_copy(
                out=a_csT.ap[:, ci * 128:(ci + 1) * 128], in_=pB[1][:, 0:128])),
                pB[1].k, a_csT.k, f"acsTe{ci}")
        for j, dst in enumerate([exp_tc, exp_gc]):
            for ci in range(nch):
                r.add("sp_q", (lambda e, j=j, ci=ci: e.dma_start(
                    out=row_st.ap,
                    in_=io["tcg_scr"][j * nch + ci:j * nch + ci + 1, :])),
                    [f"tcg{j}"], row_st.k, f"rowld{j}_{ci}")
                r.add("pe", (lambda e: e.matmul(pB[1][:, 0:128], ones_row.ap,
                                                row_st.ap, start=True, stop=True)),
                      ones_row.k + row_st.k, pB[1].k, f"bctcg{j}_{ci}")
                r.add("act", (lambda e, dst=dst, ci=ci: e.activation(
                    out=dst.ap[:, ci * 128:(ci + 1) * 128], in_=pB[1][:, 0:128],
                    func=AF.Exp)), pB[1].k, dst.k, f"exptcg{j}_{ci}")
        r.add("dve", lambda e: e.tensor_scalar(out=exp_gc.ap, in0=exp_gc.ap,
                                               scalar1=sel_sb.ap, scalar2=None,
                                               op0=ALU.mult),
              exp_gc.k + sel_sb.k, exp_gc.k, "gcsel")
        if cfg.past("p2b"):
            mask01 = RA.carve(4 * (nch * 512) + G * 512 + 98816, [128, 128], F32)
            r.add("pl", lambda e: e.memset(mask01.ap, 1.0), [], mask01.k, "mk0")
            r.add("pl", (lambda e: e.affine_select(
                out=mask01.ap, in_=mask01.ap, compare_op=ALU.is_ge, fill=0.0,
                base=0, pattern=[[1, 128]], channel_multiplier=-1)),
                mask01.k, mask01.k, "mk1")

        if cfg.past("p2a"):
            hs_fm = RA.carve(pers_end, [128, INTER], BF16)
            hsT = RA.carve(pers_end + 16384, [128, INTER], BF16)
            xdec = RA.carve(pers_end + 32768, [128, INTER], BF16)
            s_stage = xdec
            s_master = RA.carve(pers_end + 49152, [128, INTER], F32)
            bc_fm = RA.carve(pers_end + 81920, [128, 2 * G * 128], BF16)
            b_T = RA.carve(pers_end + 86016, [128, G * 128], BF16)
            s_tmp = RA.carve(pers_end + 88064, [128, 512], F32)
            dec_t = RA.carve(pers_end + 90112, [128, 128], F32)
            dd_t = RA.carve(pers_end + 90624, [128, 128], F32)

            for hb in range(4):
                r.add("dve", (lambda e, hb=hb: e.memset(
                    s_master.ap[:, hb * 2048:(hb + 1) * 2048], 0.0)),
                    [], s_master.k, "s0")

            for ci in range(nch):
                for q in range(4):
                    r.add("sp_q", (lambda e, ci=ci, q=q: e.dma_start(
                        out=hs_fm[:, q * 2048:(q + 1) * 2048].rearrange(
                            "p (b t) -> p b t", t=CHK),
                        in_=io["conv_scr"].rearrange("(b p) t -> p b t", p=128)
                        [:, q * 16:(q + 1) * 16, ci * CHK:(ci + 1) * CHK])),
                        [f"conv_scr{q * 2}", f"conv_scr{q * 2 + 1}"],
                        hs_fm.sub(q * 4096, (q + 1) * 4096),
                        f"hsld{ci}.{q}")
                r.add("sp_q", (lambda e, ci=ci: e.dma_start(
                    out=bc_fm.ap.rearrange("p (b t) -> p b t", t=CHK),
                    in_=io["conv_scr"].rearrange("(b p) t -> p b t", p=128)
                    [:, 64:80, ci * CHK:(ci + 1) * CHK])),
                    ["conv_scr8", "conv_scr9"], bc_fm.k, f"bcld{ci}")
                r.add("dve", (lambda e: e.tensor_copy(out=s_stage.ap,
                                                      in_=s_master.ap)),
                      s_master.k, s_stage.k, f"sprev{ci}")
                r.add("sp_q", (lambda e, ci=ci: e.dma_start(
                    out=io["nloc_scr"][ci], in_=s_stage.ap)),
                    s_stage.k, [f"nloc_scr{ci}"], f"nlocw{ci}")
                for b2 in range(64):
                    bank = pT[b2 % 2]
                    r.add("pe", (lambda e, b2=b2, bank=bank: e.transpose(
                        bank.ap, hs_fm[:, b2 * 128:(b2 + 1) * 128], ident_bf.ap)),
                        hs_fm.sub(b2 * 256, (b2 + 1) * 256) + ident_bf.k,
                        bank.k, f"htr{ci}.{b2}")
                    r.add("dve" if b2 % 2 == 0 else "act",
                          (lambda e, b2=b2, bank=bank, d=(b2 % 2 == 0):
                           (e.tensor_copy(out=hsT[:, b2 * 128:(b2 + 1) * 128],
                                          in_=bank.ap) if d else
                            e.copy(out=hsT[:, b2 * 128:(b2 + 1) * 128],
                                   in_=bank.ap))),
                          bank.k, hsT.k, f"htre{ci}.{b2}")
                r.add("sp_q", (lambda e, ci=ci: e.dma_start(
                    out=io["hst_scr"][ci], in_=hsT.ap)),
                    hsT.k, [f"hst_scr{ci}"], f"hstw{ci}")
                r.add("sp_q", (lambda e, ci=ci: e.dma_start(
                    out=dec_t.ap, in_=io["acs_scr"][ci])),
                    [f"acs_scr{ci}"], dec_t.k, f"decl{ci}")
                r.add("dve", (lambda e, ci=ci: e.tensor_scalar(
                    out=dec_t.ap, in0=dec_t.ap,
                    scalar1=tc_cols.ap[:, ci:ci + 1], scalar2=None,
                    op0=ALU.subtract)),
                    dec_t.k + tc_cols.k, dec_t.k, f"dec{ci}")
                r.add("act", (lambda e: e.activation(out=dec_t.ap, in_=dec_t.ap,
                                                     func=AF.Exp, scale=-1.0)),
                      dec_t.k, dec_t.k, f"dece{ci}")
                r.add("pe", (lambda e: e.transpose(pB[0][:, 0:128], dec_t.ap,
                                                   ident_f.ap)),
                      dec_t.k + ident_f.k, pB[0].k, f"dtr{ci}")
                r.add("dve", (lambda e: e.tensor_copy(out=dec_t.ap,
                                                      in_=pB[0][:, 0:128])),
                      pB[0].k, dec_t.k, f"dtre{ci}")
                r.add("dve", (lambda e, ci=ci: e.tensor_tensor(
                    out=dd_t.ap, in0=dtT[:, ci * 128:(ci + 1) * 128],
                    in1=dec_t.ap, op=ALU.mult)),
                    dtT.k + dec_t.k, dd_t.k, f"dd{ci}")
                r.add("dve", (lambda e: e.tensor_tensor(
                    out=xdec.ap.rearrange("p (h q) -> p h q", q=PDIM),
                    in0=hsT.ap.rearrange("p (h q) -> p h q", q=PDIM),
                    in1=bcast(dd_t.ap, 128, PDIM), op=ALU.mult)),
                    hsT.k + dd_t.k, xdec.k, f"xdec{ci}")
                for g in range(G):
                    bank = pT[g % 2]
                    r.add("pe", (lambda e, g=g, bank=bank: e.transpose(
                        bank.ap, bc_fm[:, g * 128:(g + 1) * 128], ident_bf.ap)),
                        bc_fm.sub(g * 256, (g + 1) * 256) + ident_bf.k,
                        bank.k, f"btr{ci}.{g}")
                    r.add("dve", (lambda e, g=g, bank=bank: e.tensor_copy(
                        out=b_T[:, g * 128:(g + 1) * 128], in_=bank.ap)),
                        bank.k, b_T.k, f"btre{ci}.{g}")
                for hb in range(16):
                    bank = pA[hb % 2]

                    def smm(e, hb=hb, bank=bank):
                        for h2 in range(8):
                            h = hb * 8 + h2
                            ins = e.matmul(
                                bank.ap[:, h2 * 64:(h2 + 1) * 64],
                                b_T[:, (h % G) * 128:(h % G + 1) * 128],
                                xdec[:, h * 64:(h + 1) * 64],
                                start=(h2 == 0), stop=(h2 == 7))
                        return ins
                    r.add("pe", smm, b_T.k + xdec.k, bank.k, f"smm{ci}.{hb}")
                    r.add("dve", (lambda e, ci=ci, hb=hb: e.tensor_tensor(
                        out=s_tmp.ap.rearrange("p (h q) -> p h q", q=PDIM),
                        in0=s_master.ap[:, hb * 512:(hb + 1) * 512].rearrange(
                            "p (h q) -> p h q", q=PDIM),
                        in1=bcast(exp_tc.ap[:, ci * 128 + hb * 8:
                                            ci * 128 + (hb + 1) * 8], 8, PDIM),
                        op=ALU.mult)),
                        s_master.k + exp_tc.k, s_tmp.k, f"sdecay{ci}.{hb}")
                    r.add("dve", (lambda e, hb=hb, bank=bank: e.tensor_tensor(
                        out=s_master.ap[:, hb * 512:(hb + 1) * 512],
                        in0=s_tmp.ap, in1=bank.ap, op=ALU.add)),
                        s_tmp.k + bank.k, s_master.k, f"supd{ci}.{hb}")

        if cfg.past("ex"):
            r.add("dve", lambda e: e.tensor_copy(out=s_stage.ap, in_=s_master.ap),
                  s_master.k, s_stage.k, "sfin")
            r.add("sp_q", lambda e: e.dma_start(out=io["snd"][:], in_=s_stage.ap),
                  s_stage.k, ["snd"], "sndw")
            r.add("pl_cc", lambda e: e.collective_compute(
                "AllGather", ALU.bypass,
                replica_groups=[list(range(cfg.ncores))],
                ins=[io["snd"].opt()], outs=[io["gth"].opt()]),
                ["snd"], ["gth"], "allgather")
            r.add("pl_q", lambda e: e.indirect_dma_start(
                out=s_stage.ap, out_offset=None, in_=io["gth"][:],
                in_offset=bass.IndirectOffsetOnAxis(ap=idx_sb.ap, axis=0),
                bounds_check=cfg.ncores * 128),
                ["gth"] + idx_sb.k, s_stage.k, "gather")
            r.add("sp_q", lambda e: e.dma_start(out=io["s_in_scr"][:],
                                                in_=s_stage.ap),
                  s_stage.k, ["s_in_scr"], "s_in_w")

        if cfg.past("p2b"):
            r_c = RA.carve(pers_end, [128, INTER], BF16)
            hsT2 = RA.carve(pers_end + 16384, [128, INTER], BF16)
            x_ch = RA.carve(pers_end + 32768, [128, INTER], BF16)
            gate_ch = RA.carve(pers_end + 49152, [128, INTER], BF16)
            xg = RA.carve(pers_end + 65536, [128, INTER], BF16)
            nloc_t = RA.carve(pers_end + 81920, [128, INTER], BF16)
            mt_b = RA.carve(pers_end + 81920, [128, 2048], BF16)
            acs_t1_b = RA.carve(pers_end + 86016, [128, 2048], F32)
            expacsT = RA.carve(pers_end + 98304, [128, 128], F32)
            normed_st = RB.carve(0, [128, INTER], BF16)
            mt = RB.carve(16384, [128, 2048], BF16)
            et_d2 = [RB.carve(20480, [128, 2048], BF16),
                     RB.carve(24576, [128, 2048], BF16)]
            acs_t1 = RB.carve(28672, [128, 2048], F32)
            bc2 = RB.carve(36864, [128, 2 * G * 128], BF16)
            y_half_d = [RB.carve(40960, [128, 512], F32),
                        RB.carve(43008, [128, 512], F32)]
            sq_st = RB.carve(47104, [128, 1024], F32)
            rbc = RB.carve(47104, [128, 1024], F32)
            rstd = RB.carve(51200, [1, 1024], F32)

            mt_d = [mt, mt_b]
            acs_d = [acs_t1, acs_t1_b]

            def load_chunk(ci):
                for q in range(4):
                    r.add("sp_q", (lambda e, ci=ci, q=q: e.dma_start(
                        out=gate_ch[:, q * 2048:(q + 1) * 2048].rearrange(
                            "p (b t) -> p b t", t=CHK),
                        in_=io["gate_scr"].rearrange("(b p) t -> p b t", p=128)
                        [:, q * 16:(q + 1) * 16, ci * CHK:(ci + 1) * CHK])),
                        [f"gate_scr{q * 2}", f"gate_scr{q * 2 + 1}"],
                        gate_ch.sub(q * 4096, (q + 1) * 4096), f"gld{ci}.{q}")
                r.add("sp_q", (lambda e, ci=ci: e.dma_start(
                    out=hsT2.ap, in_=io["hst_scr"][ci])),
                    [f"hst_scr{ci}"], hsT2.k, f"hld{ci}")
                r.add("sp_q", (lambda e, ci=ci: e.dma_start(
                    out=bc2.ap.rearrange("p (b t) -> p b t", t=CHK),
                    in_=io["conv_scr"].rearrange("(b p) t -> p b t", p=128)
                    [:, 64:80, ci * CHK:(ci + 1) * CHK])),
                    ["conv_scr8", "conv_scr9"], bc2.k, f"bld{ci}")
                r.add("pl_q", (lambda e: e.dma_start(out=r_c.ap,
                                                     in_=io["s_in_scr"][:])),
                      ["s_in_scr"], r_c.k, f"rld{ci}")
                r.add("pl_q", (lambda e, ci=ci: e.dma_start(
                    out=nloc_t.ap, in_=io["nloc_scr"][ci])),
                    [f"nloc_scr{ci}"], nloc_t.k, f"nlld{ci}")

            def prep_chunk(ci):
                if cfg.p2b_cut >= 12:
                    r.add("pl", (lambda e, ci=ci: e.tensor_tensor(
                        out=r_c.ap.rearrange("p (h q) -> p h q", q=PDIM),
                        in0=r_c.ap.rearrange("p (h q) -> p h q", q=PDIM),
                        in1=bcast(exp_gc.ap[:, ci * 128:(ci + 1) * 128], 128,
                                  PDIM),
                        op=ALU.mult)),
                        r_c.k + exp_gc.k, r_c.k, f"rgam{ci}")
                if cfg.p2b_cut >= 14:
                    r.add("pl_q", (lambda e, ci=ci: e.dma_start(
                        out=nloc_t.ap, in_=io["nloc_scr"][ci])),
                        [f"nloc_scr{ci}"], nloc_t.k, f"nlld{ci}")
                    r.add("pl", (lambda e: e.tensor_tensor(
                        out=r_c.ap, in0=r_c.ap, in1=nloc_t.ap, op=ALU.add)),
                        r_c.k + nloc_t.k, r_c.k, f"radd{ci}")
                if cfg.p2b_cut >= 16:
                    r.add("dve", (lambda e, ci=ci: e.tensor_tensor(
                        out=x_ch.ap.rearrange("p (h q) -> p h q", q=PDIM),
                        in0=hsT2.ap.rearrange("p (h q) -> p h q", q=PDIM),
                        in1=bcast(dtT.ap[:, ci * 128:(ci + 1) * 128], 128, PDIM),
                        op=ALU.mult)),
                        hsT2.k + dtT.k, x_ch.k, f"xch{ci}")
                    r.add("dve", (lambda e: e.tensor_tensor(
                        out=hsT2.ap.rearrange("p (h q) -> p h q", q=PDIM),
                        in0=hsT2.ap.rearrange("p (h q) -> p h q", q=PDIM),
                        in1=bcast(dbc.ap[:, 0:128], 128, PDIM),
                        op=ALU.mult)),
                        hsT2.k + dbc.k, hsT2.k, f"hsd{ci}")
                if cfg.p2b_cut >= 18:
                    r.add("act", (lambda e, ci=ci: e.activation(
                        out=expacsT.ap, in_=a_csT[:, ci * 128:(ci + 1) * 128],
                        func=AF.Exp)), a_csT.k, expacsT.k, f"eat{ci}")
                    r.add("act", (lambda e: e.activation(out=gate_ch.ap,
                                                         in_=gate_ch.ap,
                                                         func=AF.Silu)),
                          gate_ch.k, gate_ch.k, f"sgate{ci}")
                if cfg.p2b_cut < 20:
                    return
                for g in range(G):
                    bank = pB[g % 2]
                    r.add("pe", (lambda e, g=g, bank=bank: e.matmul(
                        bank.ap[:, 0:128], bc2[:, g * 128:(g + 1) * 128],
                        bc2[:, (G + g) * 128:(G + g + 1) * 128],
                        start=True, stop=True)),
                        bc2.k, bank.k, f"gm{ci}.{g}")
                    r.add("act", (lambda e, g=g, bank=bank: e.copy(
                        out=gm_all.ap[:, g * 128:(g + 1) * 128],
                        in_=bank.ap[:, 0:128])),
                        bank.k, gm_all.k, f"gme{ci}.{g}")
                r.add("dve", (lambda e: e.tensor_tensor(
                    out=gm_all.ap.rearrange("p (g l) -> p g l", l=128),
                    in0=gm_all.ap.rearrange("p (g l) -> p g l", l=128),
                    in1=mask01.ap.unsqueeze(1).broadcast_to((128, G, 128)),
                    op=ALU.mult)),
                    gm_all.k + mask01.k, gm_all.k, f"gmmask{ci}")

            load_chunk(0)
            prep_chunk(0)
            for ci in range(nch):
                def chainE(hg, ci=ci):
                    mt_c = mt_d[hg % 2]
                    acs_c = acs_d[hg % 2]
                    etile = et_d2[hg % 2]
                    r.add("sp_q", (lambda e, ci=ci, hg=hg, acs_c=acs_c:
                                   e.dma_start(
                        out=acs_c.ap,
                        in_=io["acs_scr"][ci][hg * 16:(hg + 1) * 16, :]
                        .rearrange("h l -> (h l)").unsqueeze(0)
                        .broadcast_to((128, 2048)))),
                        [f"acs_scr{ci}"], acs_c.k, f"t1ld{ci}.{hg}")
                    r.add("pl", (lambda e, ci=ci, hg=hg, acs_c=acs_c,
                                 etile=etile: e.tensor_tensor(
                        out=etile.ap.rearrange("p (h l) -> p h l", l=128),
                        in0=acs_c.ap.rearrange("p (h l) -> p h l", l=128),
                        in1=bcast(a_csT.ap[:, ci * 128 + hg * 16:
                                           ci * 128 + (hg + 1) * 16], 16, 128),
                        op=ALU.subtract)),
                        acs_c.k + a_csT.k, etile.k, f"esub{ci}.{hg}")
                    r.add("act", (lambda e, etile=etile: e.activation(
                        out=etile.ap, in_=etile.ap, func=AF.Relu, scale=-1.0)),
                        etile.k, etile.k, f"eclamp{ci}.{hg}")
                    r.add("act", (lambda e, etile=etile: e.activation(
                        out=etile.ap, in_=etile.ap, func=AF.Exp, scale=-1.0)),
                        etile.k, etile.k, f"eexp{ci}.{hg}")
                    r.add("dve", (lambda e, hg=hg, mt_c=mt_c, etile=etile:
                                  e.tensor_tensor(
                        out=mt_c.ap.rearrange("p (r g l) -> p r g l", g=G, l=128),
                        in0=etile.ap.rearrange("p (r g l) -> p r g l", g=G, l=128),
                        in1=gm_all.ap.rearrange("p (g l) -> p g l", l=128)
                        .unsqueeze(1).broadcast_to((128, 2, G, 128)),
                        op=ALU.mult)),
                        etile.k + gm_all.k, mt_c.k, f"emul{ci}.{hg}")

                def mmhalf(hh, ci=ci):
                    hg, half = hh // 2, hh % 2
                    mt_c = mt_d[hg % 2]
                    bY = pA[2 * (hh % 2)]
                    bO = pA[2 * (hh % 2) + 1]

                    def ymm(e, hg=hg, half=half, mt_c=mt_c, bY=bY):
                        for j in range(8):
                            h = hg * 16 + half * 8 + j
                            jj = half * 8 + j
                            ins = e.matmul(bY.ap[:, j * 64:(j + 1) * 64],
                                           mt_c[:, jj * 128:(jj + 1) * 128],
                                           x_ch[:, h * 64:(h + 1) * 64],
                                           start=(j == 0), stop=False)
                        for j in range(8):
                            h = hg * 16 + half * 8 + j
                            ins = e.matmul(bY.ap[:, j * 64:(j + 1) * 64],
                                           ident_bf.ap,
                                           hsT2[:, h * 64:(h + 1) * 64],
                                           start=False, stop=(j == 7))
                        return ins
                    r.add("pe", ymm, mt_c.k + x_ch.k + hsT2.k + ident_bf.k,
                          bY.k, f"ydmm{ci}.{hh}")

                    def omm(e, hg=hg, half=half, bO=bO):
                        for j in range(8):
                            h = hg * 16 + half * 8 + j
                            ins = e.matmul(bO.ap[:, j * 64:(j + 1) * 64],
                                           bc2[:, (G + h % G) * 128:
                                               (G + h % G + 1) * 128],
                                           r_c[:, h * 64:(h + 1) * 64],
                                           start=(j == 0), stop=(j == 7))
                        return ins
                    r.add("pe", omm, bc2.k + r_c.k, bO.k, f"yomm{ci}.{hh}")

                def evhalf(hh, ci=ci):
                    hg, half = hh // 2, hh % 2
                    h0 = hg * 16 + half * 8
                    bY = pA[2 * (hh % 2)]
                    bO = pA[2 * (hh % 2) + 1]
                    y_half = y_half_d[hh % 2]
                    r.add("dve", (lambda e, h0=h0, bO=bO, y_half=y_half:
                                  e.tensor_tensor(
                        out=y_half.ap.rearrange("p (h q) -> p h q", q=PDIM),
                        in0=bO.ap.rearrange("p (h q) -> p h q", q=PDIM),
                        in1=bcast(expacsT.ap[:, h0:h0 + 8], 8, PDIM),
                        op=ALU.mult)),
                        bO.k + expacsT.k, y_half.k, f"yo{ci}.{h0}")
                    r.add("dve", (lambda e, bY=bY, y_half=y_half:
                                  e.tensor_tensor(
                        out=y_half.ap, in0=y_half.ap, in1=bY.ap, op=ALU.add)),
                        y_half.k + bY.k, y_half.k, f"ya{ci}.{h0}")
                    for j in range(4):
                        r.add("pe", (lambda e, j=j, half=half, y_half=y_half:
                                     e.transpose(
                            pB[half].ap[:, j * 128:(j + 1) * 128],
                            y_half[:, j * 128:(j + 1) * 128], ident_f.ap)),
                            y_half.k + ident_f.k, pB[half].k,
                            f"ytr{ci}.{h0}.{j}")
                    blk0 = (h0 * 64) // 128
                    r.add("dve", (lambda e, half=half, blk0=blk0:
                                  e.tensor_tensor(
                        out=xg[:, blk0 * 128:(blk0 + 4) * 128],
                        in0=pB[half].ap,
                        in1=gate_ch[:, blk0 * 128:(blk0 + 4) * 128],
                        op=ALU.mult)),
                        pB[half].k + gate_ch.k, xg.k, f"xg{ci}.{h0}")

                if ci > 0:
                    prep_chunk(ci)
                chainE(0)
                for hh in range(16):
                    if hh % 2 == 0 and hh // 2 + 1 < 8:
                        chainE(hh // 2 + 1)
                    mmhalf(hh)
                    if hh >= 1:
                        evhalf(hh - 1)
                evhalf(15)
                if ci + 1 < nch:
                    load_chunk(ci + 1)
                if cfg.p2b_cut < 60:
                    continue
                for g in range(G):
                    r.add("act", (lambda e, g=g: e.activation(
                        out=sq_st.ap, in_=xg[:, g * 1024:(g + 1) * 1024],
                        func=AF.Square)), xg.k, sq_st.k, f"sq{ci}.{g}")

                    def ssum(e, g=g):
                        bank = pB[g // 4]
                        for pb in range(8):
                            ins = e.matmul(
                                bank.ap[0:1, (g % 4) * 128:(g % 4 + 1) * 128],
                                ones_col.ap,
                                sq_st[:, pb * 128:(pb + 1) * 128],
                                start=(pb == 0), stop=(pb == 7))
                        return ins
                    r.add("pe", ssum, sq_st.k + ones_col.k, pB[g // 4].k,
                          f"ss{ci}.{g}")
                for j in range(2):
                    r.add("act", (lambda e, j=j: e.copy(
                        out=rstd.ap[0:1, j * 512:(j + 1) * 512],
                        in_=pB[j].ap[0:1, 0:512])),
                        pB[j].k, rstd.k, f"rse{ci}.{j}")
                r.add("dve", (lambda e: e.tensor_scalar(
                    out=rstd.ap, in0=rstd.ap, scalar1=1.0 / 1024.0, scalar2=EPS,
                    op0=ALU.mult, op1=ALU.add)),
                    rstd.k, rstd.k, f"rs1{ci}")
                r.add("act", (lambda e: e.activation(out=rstd.ap, in_=rstd.ap,
                                                     func=AF.Sqrt)),
                      rstd.k, rstd.k, f"rs2{ci}")
                r.add("dve", (lambda e: e.reciprocal(out=rstd.ap, in_=rstd.ap)),
                      rstd.k, rstd.k, f"rs3{ci}")
                for g in range(G):
                    bank = pB[g % 2]
                    r.add("pe", (lambda e, g=g, bank=bank: e.matmul(
                        bank.ap[:, 0:128], ones_row.ap,
                        rstd.ap[0:1, g * 128:(g + 1) * 128],
                        start=True, stop=True)),
                        ones_row.k + rstd.k, bank.k, f"rbc{ci}.{g}")
                    r.add("dve", (lambda e, g=g, bank=bank: e.tensor_copy(
                        out=rbc.ap[:, g * 128:(g + 1) * 128],
                        in_=bank.ap[:, 0:128])),
                        bank.k, rbc.k, f"rbce{ci}.{g}")
                for g in range(G):
                    r.add("dve", (lambda e, g=g: e.tensor_tensor(
                        out=normed_st[:, g * 1024:(g + 1) * 1024].rearrange(
                            "p (b t) -> p b t", t=128),
                        in0=xg[:, g * 1024:(g + 1) * 1024].rearrange(
                            "p (b t) -> p b t", t=128),
                        in1=rbc.ap[:, g * 128:(g + 1) * 128].unsqueeze(1)
                        .broadcast_to((128, 8, 128)),
                        op=ALU.mult)),
                        xg.k + rbc.k, normed_st.k, f"nm1{ci}.{g}")
                    r.add("dve", (lambda e, g=g: e.tensor_tensor(
                        out=normed_st[:, g * 1024:(g + 1) * 1024].rearrange(
                            "p (b t) -> p b t", t=128),
                        in0=normed_st[:, g * 1024:(g + 1) * 1024].rearrange(
                            "p (b t) -> p b t", t=128),
                        in1=bcast(nw_sb.ap[:, g * 8:(g + 1) * 8], 8, 128),
                        op=ALU.mult)),
                        normed_st.k + nw_sb.k, normed_st.k, f"nm2{ci}.{g}")
                for q in range(4):
                    r.add("sp_q", (lambda e, ci=ci, q=q: e.dma_start(
                        out=io["normed_scr"].rearrange("(b p) t -> p b t", p=128)
                        [:, q * 16:(q + 1) * 16, ci * CHK:(ci + 1) * CHK],
                        in_=normed_st[:, q * 2048:(q + 1) * 2048].rearrange(
                            "p (b t) -> p b t", t=CHK))),
                        normed_st.k, [f"normed_scr{ci}"], f"nw{ci}.{q}")

        if cfg.past("p3"):
            nstrip3 = t_own // 512
            np3 = RA.carve(0, [128, 64 * 512], BF16)
            out3 = [RA.carve(65536 + i * 16384, [128, HID], F32) for i in range(4)]
            w_p3 = [RA.carve(131072 + i * 4096, [128, 4 * 512], BF16)
                    for i in range(2)]
            xres = RB.carve(0, [128, HID], F32)
            ln_bf = RB.carve(16384, [128, HID], BF16)
            lnfm_st = RB.carve(24576, [128, NKB * 128], BF16)

            for s3 in range(nstrip3):
                tok0 = s3 * 512
                for q in range(8):
                    r.add("sp_q" if q % 2 == 0 else "act_q",
                          (lambda e, tok0=tok0, q=q: e.dma_start(
                        out=np3[:, q * 8 * 512:(q + 1) * 8 * 512].rearrange(
                            "p (b t) -> p b t", t=512),
                        in_=io["normed_scr"].rearrange("(b p) t -> p b t", p=128)
                        [:, q * 8:(q + 1) * 8, tok0:tok0 + 512])),
                        [f"normed_scr{i}" for i in range(tok0 // CHK,
                                                         tok0 // CHK + 4)],
                        np3.sub(q * 8192, (q + 1) * 8192), f"npld{s3}.{q}")
                for og in range(8):
                    bsel = [pA[0] if og % 2 == 0 else pB[0],
                            pA[1] if og % 2 == 0 else pB[1], pA[2], pA[3]]
                    for cq in range(16):
                        slot = cq % 2
                        r.add("act_q" if cq % 2 == 0 else "sp_q",
                              (lambda e, og=og, cq=cq, slot=slot:
                                        e.dma_start(
                            out=w_p3[slot].ap.rearrange("p (c o) -> p c o", o=512),
                            in_=io["wout_bf"][cq * 4:(cq + 1) * 4, :,
                                              og * 512:(og + 1) * 512]
                            .rearrange("c p o -> p c o"))),
                            [f"wout_bf{(cq * 4) // 8 * 8}"], w_p3[slot].k,
                            f"wp3{s3}.{og}.{cq}")

                        def p3mm(e, og=og, cq=cq, slot=slot, bsel=bsel):
                            for c4 in range(4):
                                chb = cq * 4 + c4
                                for tb in range(4):
                                    ins = e.matmul(
                                        bsel[tb].ap,
                                        np3[:, chb * 512 + tb * 128:
                                            chb * 512 + (tb + 1) * 128],
                                        w_p3[slot][:, c4 * 512:(c4 + 1) * 512],
                                        start=(cq == 0 and c4 == 0),
                                        stop=(cq == 15 and c4 == 3))
                            return ins
                        r.add("pe", p3mm,
                              np3.sub(cq * 4096, (cq + 1) * 4096)
                              + w_p3[slot].k,
                              [k for tb in range(4) for k in bsel[tb].k],
                              f"p3mm{s3}.{og}.{cq}")
                    for tb in range(4):
                        r.add("act", (lambda e, og=og, tb=tb, bsel=bsel: e.copy(
                            out=out3[tb].ap[:, og * 512:(og + 1) * 512],
                            in_=bsel[tb].ap)),
                            bsel[tb].k, out3[tb].k, f"p3ev{s3}.{og}.{tb}")
                for tb in range(4):
                    trow = tok0 + tb * 128
                    r.add("sp_q", (lambda e, trow=trow: e.dma_start(
                        out=xres.ap, in_=io["x_pad"][halo + trow:
                                                     halo + trow + 128, :])),
                        [], xres.k, f"xres{s3}.{tb}")
                    emit_ln(r, cfg, out3[tb], xres, stats, io, trow,
                            "ln1tm_scr", f"ln1tm{trow // 128}", f"l1_{trow}",
                            ident_bf, ln_bf, lnfm_st, pT, io_fm="ln1fm_scr",
                            fm_key=f"ln1fm{trow // 128}")

        if cfg.past("p4"):
            nstrip4 = t_own // 512
            xln4 = RA.carve(0, [128, NKB * 512], BF16)
            h4 = RA.carve(32768, [128, 32 * 512], BF16)
            w12sb = [RA.carve(65536 + i * 16384, [128, NKB * 256], BF16)
                     for i in range(2)]
            b3bc = RA.carve(98304, [128, HID], F32)
            out4 = [RA.carve(114688, [128, HID], F32),
                    RA.carve(81920, [128, HID], F32)]
            b3st = RA.carve(114688, [1, HID], F32)
            w3sb = [RA.carve(131072 + i * 4096, [128, 4 * 512], BF16)
                    for i in range(2)]
            lnres = RB.carve(0, [128, HID], F32)
            h1s = RB.carve(16384, [128, 512], BF16)
            h2t = RB.carve(17408, [128, 512], F32)

            r.add("sp_q", lambda e: e.dma_start(out=b3st.ap, in_=io["b3_row"][:]),
                  [], b3st.k, "b3ld")
            for j in range(8):
                r.add("pe", (lambda e, j=j: e.matmul(
                    pB[j % 2].ap, ones_row.ap,
                    b3st.ap[0:1, j * 512:(j + 1) * 512], start=True, stop=True)),
                    ones_row.k + b3st.k, pB[j % 2].k, f"b3mm{j}")
                r.add("dve", (lambda e, j=j: e.tensor_copy(
                    out=b3bc.ap[:, j * 512:(j + 1) * 512], in_=pB[j % 2].ap)),
                    pB[j % 2].k, b3bc.k, f"b3ev{j}")

            for s4 in range(nstrip4):
                tok0 = s4 * 512
                for q in range(2):
                    r.add("sp_q", (lambda e, tok0=tok0, q=q: e.dma_start(
                        out=xln4[:, q * 16 * 512:(q + 1) * 16 * 512].rearrange(
                            "p (b t) -> p b t", t=512),
                        in_=io["ln1fm_scr"].rearrange("(b p) t -> p b t", p=128)
                        [:, q * 16:(q + 1) * 16, tok0:tok0 + 512])),
                        [f"ln1fm{i}" for i in range(tok0 // 128, tok0 // 128 + 4)],
                        xln4.sub(q * 16384, (q + 1) * 16384), f"xlnld{s4}.{q}")
                for pr in range(32):
                    slot = pr % 2
                    for q in range(2):
                        r.add("act_q" if q == 0 else "sp_q",
                              (lambda e, pr=pr, slot=slot, q=q:
                                        e.dma_start(
                            out=w12sb[slot][:, q * 16 * 256:(q + 1) * 16 * 256]
                            .rearrange("p (c w) -> p c w", w=256),
                            in_=io["w12_bf"][pr][q * 16:(q + 1) * 16]
                            .rearrange("c p w -> p c w"))),
                            [f"w12_bf{pr // 4 * 4}"], w12sb[slot].k,
                            f"w12ld{s4}.{pr}.{q}")
                    for half in range(2):
                        def w12mm(e, pr=pr, slot=slot, half=half):
                            for chb in range(NKB):
                                ins = e.matmul(
                                    pA[half].ap,
                                    w12sb[slot][:, chb * 256 + half * 128:
                                                chb * 256 + half * 128 + 128],
                                    xln4[:, chb * 512:(chb + 1) * 512],
                                    start=(chb == 0), stop=(chb == NKB - 1))
                            return ins
                        r.add("pe", w12mm, w12sb[slot].k + xln4.k, pA[half].k,
                              f"w12mm{s4}.{pr}.{half}")
                    r.add("act", (lambda e, pr=pr: e.activation(
                        out=h1s.ap, in_=pA[0].ap, func=AF.Silu,
                        bias=b12_sb.ap[:, pr:pr + 1], scale=1.0)),
                        pA[0].k + b12_sb.k, h1s.k, f"h1s{s4}.{pr}")
                    r.add("dve", (lambda e, pr=pr: e.tensor_scalar(
                        out=h2t.ap, in0=pA[1].ap,
                        scalar1=b12_sb.ap[:, 32 + pr:33 + pr], scalar2=None,
                        op0=ALU.add)),
                        pA[1].k + b12_sb.k, h2t.k, f"h2{s4}.{pr}")
                    r.add("dve", (lambda e, pr=pr: e.tensor_tensor(
                        out=h4[:, pr * 512:(pr + 1) * 512], in0=h1s.ap,
                        in1=h2t.ap, op=ALU.mult)),
                        h1s.k + h2t.k, h4.k, f"h4{s4}.{pr}")
                for sub in range(2):
                    for og in range(8):
                        b4 = [pA[2], pA[3]] if og % 2 == 0 else [pB[0], pB[1]]
                        for mq in range(8):
                            slot = mq % 2
                            r.add("act_q" if mq % 2 == 0 else "sp_q",
                                  (lambda e, og=og, mq=mq, slot=slot:
                                            e.dma_start(
                                out=w3sb[slot].ap.rearrange("p (c o) -> p c o",
                                                            o=512),
                                in_=io["w3_bf"][mq * 4:(mq + 1) * 4, :,
                                                og * 512:(og + 1) * 512]
                                .rearrange("c p o -> p c o"))),
                                [f"w3_bf{(mq * 4) // 8 * 8}"], w3sb[slot].k,
                                f"w3ld{s4}.{sub}.{og}.{mq}")

                            def w3mm(e, og=og, mq=mq, slot=slot, sub=sub,
                                     b4=b4):
                                for m4 in range(4):
                                    mb = mq * 4 + m4
                                    for tb in range(2):
                                        ins = e.matmul(
                                            b4[tb].ap,
                                            h4[:, mb * 512 + sub * 256 +
                                               tb * 128:
                                               mb * 512 + sub * 256 +
                                               (tb + 1) * 128],
                                            w3sb[slot][:, m4 * 512:
                                                       (m4 + 1) * 512],
                                            start=(mq == 0 and m4 == 0),
                                            stop=(mq == 7 and m4 == 3))
                                return ins
                            r.add("pe", w3mm, h4.k + w3sb[slot].k,
                                  b4[0].k + b4[1].k, f"w3mm{s4}.{sub}.{og}.{mq}")
                        for tb in range(2):
                            r.add("dve", (lambda e, og=og, tb=tb, b4=b4:
                                          e.tensor_tensor(
                                out=out4[tb].ap[:, og * 512:(og + 1) * 512],
                                in0=b4[tb].ap,
                                in1=b3bc.ap[:, og * 512:(og + 1) * 512],
                                op=ALU.add)),
                                b4[tb].k + b3bc.k, out4[tb].k,
                                f"p4ev{s4}.{sub}.{og}.{tb}")
                    for tb in range(2):
                        trow = tok0 + sub * 256 + tb * 128
                        r.add("sp_q", (lambda e, trow=trow: e.dma_start(
                            out=lnres.ap, in_=io["ln1tm_scr"][trow:trow + 128, :])),
                            [f"ln1tm{trow // 128}"], lnres.k, f"lrld{s4}.{trow}")
                        emit_ln(r, cfg, out4[tb], lnres, stats, io, trow,
                                "out", f"out{trow // 128}", f"l2_{trow}",
                                ident_bf, None, None, pT)

        fence_keys = []
        if cfg.past("p4"):
            fence_keys += [f"out{i}" for i in range(t_own // 128)]
        for name, dst in taps.items():
            src = io[name]
            nrow = src.shape[0]
            inner = int(np.prod(src.shape[1:])) if len(src.shape) > 1 else 1
            esz = 2 if src.dtype == BF16 else 4
            step = max(1, min(nrow, (1 << 22) // max(1, inner * esz)))
            keys = all_keys_of(name, cfg)
            i0 = 0
            while i0 < nrow:
                i1 = min(nrow, i0 + step)
                r.add("sp_q", (lambda e, name=name, dst=dst, i0=i0, i1=i1:
                               e.dma_start(out=dst[i0:i1], in_=io[name][i0:i1])),
                      keys, [f"tapw_{name}"], f"tap_{name}")
                i0 = i1
            fence_keys.append(f"tapw_{name}")
        r.add("pl", lambda e: e.nop(), fence_keys, [], "fence")

        stats_ = r.assign_and_emit(nc, block, st)
    return nc, stats_


def emit_ln(r, cfg, src, res, stats, io, trow, out_name, out_key, tag,
            ident_bf, ln_bf, lnfm_st, pT, io_fm=None, fm_key=None):
    ALU_ = ALU
    r.add("dve", (lambda e: e.tensor_tensor(out=src.ap, in0=src.ap, in1=res.ap,
                                            op=ALU_.add)),
          src.k + res.k, src.k, f"{tag}_res")
    r.add("act", (lambda e: e.activation(out=res.ap, in_=src.ap,
                                         func=AF.Identity,
                                         accum_out=stats.ap[:, 0:1])),
          src.k, res.k + stats.k, f"{tag}_sum")
    r.add("act", (lambda e: e.activation(out=res.ap, in_=src.ap, func=AF.Square,
                                         accum_out=stats.ap[:, 1:2])),
          src.k, res.k + stats.k, f"{tag}_sq")
    r.add("dve", (lambda e: e.tensor_scalar(out=stats.ap[:, 0:1],
                                            in0=stats.ap[:, 0:1],
                                            scalar1=1.0 / HID, scalar2=None,
                                            op0=ALU_.mult)),
          stats.k, stats.k, f"{tag}_s1")
    r.add("dve", (lambda e: e.tensor_tensor(out=stats.ap[:, 2:3],
                                            in0=stats.ap[:, 0:1],
                                            in1=stats.ap[:, 0:1],
                                            op=ALU_.mult)),
          stats.k, stats.k, f"{tag}_s2")
    r.add("dve", (lambda e: e.tensor_scalar(out=stats.ap[:, 1:2],
                                            in0=stats.ap[:, 1:2],
                                            scalar1=1.0 / HID, scalar2=None,
                                            op0=ALU_.mult)),
          stats.k, stats.k, f"{tag}_s3")
    r.add("dve", (lambda e: e.tensor_tensor(out=stats.ap[:, 3:4],
                                            in0=stats.ap[:, 1:2],
                                            in1=stats.ap[:, 2:3],
                                            op=ALU_.subtract)),
          stats.k, stats.k, f"{tag}_s4")
    r.add("dve", (lambda e: e.tensor_scalar(out=stats.ap[:, 3:4],
                                            in0=stats.ap[:, 3:4],
                                            scalar1=EPS, scalar2=None,
                                            op0=ALU_.add)),
          stats.k, stats.k, f"{tag}_s4b")
    r.add("act", (lambda e: e.activation(out=stats.ap[:, 4:5],
                                         in_=stats.ap[:, 3:4], func=AF.Sqrt)),
          stats.k, stats.k, f"{tag}_s5")
    r.add("dve", (lambda e: e.reciprocal(out=stats.ap[:, 5:6],
                                         in_=stats.ap[:, 4:5])),
          stats.k, stats.k, f"{tag}_s6")
    r.add("dve", (lambda e: e.tensor_scalar(out=stats.ap[:, 6:7],
                                            in0=stats.ap[:, 0:1],
                                            scalar1=-1.0, scalar2=None,
                                            op0=ALU_.mult)),
          stats.k, stats.k, f"{tag}_s7")
    r.add("dve", (lambda e: e.tensor_scalar(out=src.ap, in0=src.ap,
                                            scalar1=stats.ap[:, 6:7],
                                            scalar2=stats.ap[:, 5:6],
                                            op0=ALU_.add, op1=ALU_.mult)),
          src.k + stats.k, src.k, f"{tag}_nrm")
    r.add("sp_q", (lambda e: e.dma_start(out=io[out_name][trow:trow + 128, :],
                                         in_=src.ap)),
          src.k, [out_key], f"{tag}_w")
    if io_fm is not None:
        r.add("dve", (lambda e: e.tensor_copy(out=ln_bf.ap, in_=src.ap)),
              src.k, ln_bf.k, f"{tag}_bf")
        for cb in range(NKB):
            bank = pT[cb % 2]
            r.add("pe", (lambda e, cb=cb, bank=bank: e.transpose(
                bank.ap, ln_bf[:, cb * 128:(cb + 1) * 128], ident_bf.ap)),
                ln_bf.k + ident_bf.k, bank.k, f"{tag}_tr{cb}")
            r.add("dve", (lambda e, cb=cb, bank=bank: e.tensor_copy(
                out=lnfm_st.ap[:, cb * 128:(cb + 1) * 128], in_=bank.ap)),
                bank.k, lnfm_st.k, f"{tag}_tre{cb}")
        for q in range(2):
            r.add("sp_q", (lambda e, q=q: e.dma_start(
                out=io[io_fm].rearrange("(b p) t -> p b t", p=128)
                [:, q * 16:(q + 1) * 16, trow:trow + 128],
                in_=lnfm_st.ap[:, q * 16 * 128:(q + 1) * 16 * 128].rearrange(
                    "p (b t) -> p b t", t=128))),
                lnfm_st.k, [fm_key], f"{tag}_fmw{q}")


def all_keys_of(name, cfg):
    table = {
        "gate_scr": [f"gate_scr{i}" for i in range(8)],
        "conv_scr": [f"conv_scr{i}" for i in range(10)],
        "acs_scr": [f"acs_scr{i}" for i in range(cfg.nch)],
        "hst_scr": [f"hst_scr{i}" for i in range(cfg.nch)],
        "nloc_scr": [f"nloc_scr{i}" for i in range(cfg.nch)],
        "normed_scr": [f"normed_scr{i}" for i in range(cfg.nch)],
        "ln1tm_scr": [f"ln1tm{i}" for i in range(cfg.t_own // 128)],
        "ln1fm_scr": [f"ln1fm{i}" for i in range(cfg.t_own // 128)],
        "s_in_scr": ["s_in_scr"],
        "snd": ["snd"],
        "gth": ["gth"],
        "x_bf": ["x_bf"],
        "tcg_scr": ["tcg0", "tcg1"],
    }
    return table.get(name, [name])


from concourse.bass_utils import run_bass_kernel_spmd


def kernel(**inputs):
    cfg = Cfg(t_own=2048, halo=128, ncores=8, upto="p4", taps=())
    sh = host_shared(inputs)
    in_maps = [host_core_map(cfg, inputs, c, sh) for c in range(cfg.ncores)]
    nc, _ = build(cfg)
    res = run_bass_kernel_spmd(nc, in_maps, core_ids=list(range(cfg.ncores)))
    x = np.asarray(inputs["L0_attn_hidden_states"])
    B, L = x.shape[0], x.shape[1]
    out = np.empty((B, L, HID), np.float32)
    for c in range(cfg.ncores):
        seq, part = c % B, c // B
        out[seq, part * cfg.t_own:(part + 1) * cfg.t_own] = res.results[c]["out"]
    return out
